# version 20
# baseline (speedup 1.0000x reference)
import numpy as np

import concourse.bass as bass
import concourse.mybir as mybir
from concourse.bass_utils import run_bass_kernel_spmd

F32 = mybir.dt.float32
F16 = mybir.dt.float16
AF = mybir.ActivationFunctionType
ALU = mybir.AluOpType
AX = mybir.AxisListType.X

D = 2048
H = 16
I = 8192
NCD = D // 128
NCI = I // 128
N_CORES = 8
EPS = 1e-5
FWHT_A = 0.0004873637808486819
FWHT_K1 = 45.34002685546875
FWHT_K2 = 22.670013427734375


class Ctr:
    def __init__(self, handle):
        self.h = handle
        self.n = 0

    def inc(self, ins, k=1):
        ins.then_inc(self.h, k)
        self.n += k
        return self.n


class Arena:

    def __init__(self, ap, ncols):
        self.ap = ap
        self.avail = [(0, ncols)]
        self.used = {}

    def alloc(self, name, n, high=False):
        idxs = range(len(self.avail) - 1, -1, -1) if high else range(len(self.avail))
        for i in idxs:
            s, l = self.avail[i]
            if l >= n:
                if high:
                    start = s + l - n
                    self.used[name] = (start, n)
                    if l == n:
                        self.avail.pop(i)
                    else:
                        self.avail[i] = (s, l - n)
                    return self.ap[:, start:start + n]
                self.used[name] = (s, n)
                if l == n:
                    self.avail.pop(i)
                else:
                    self.avail[i] = (s + n, l - n)
                return self.ap[:, s:s + n]
        raise MemoryError(f"arena: {name} needs {n} cols; free={self.avail}")

    def release(self, *names):
        for name in names:
            s, n = self.used.pop(name)
            self.avail.append((s, n))
        self.avail.sort()
        merged = []
        for s, l in self.avail:
            if merged and merged[-1][0] + merged[-1][1] == s:
                merged[-1] = (merged[-1][0], merged[-1][1] + l)
            else:
                merged.append((s, l))
        self.avail = merged


def build_program(T):
    NT = T // 128
    HW = min(512, T)
    NH = T // HW
    nc = bass.Bass("TRN2", target_bir_lowering=False, debug=False)

    x_tm_d = nc.dram_tensor("x_tm", [T, D], F32, kind="ExternalInput").ap()
    x_fm_d = nc.dram_tensor("x_fm", [D, T], F32, kind="ExternalInput").ap()
    wv_d = nc.dram_tensor("wv", [D, D], F32, kind="ExternalInput").ap()
    wo_d = nc.dram_tensor("wo", [D, D], F32, kind="ExternalInput").ap()
    wup_d = nc.dram_tensor("wup", [NCI * 128, NCD * 128], F32, kind="ExternalInput").ap()
    wgt_d = nc.dram_tensor("wgt", [NCI * 128, NCD * 128], F32, kind="ExternalInput").ap()
    wdn_d = nc.dram_tensor("wdn", [64 * 128, 16 * 128], F32, kind="ExternalInput").ap()
    vec_d = {}
    for nm, w in (("bv", NCD), ("bo", NCD), ("suug", NCD), ("sudp", NCI),
                  ("wsu", NCI), ("wsg", NCI), ("wsd", NCD)):
        vec_d[nm] = nc.dram_tensor(nm, [128, w], F32, kind="ExternalInput").ap()
    h1_d = nc.dram_tensor("h1c", [128, 128], F32, kind="ExternalInput").ap()
    h2_d = nc.dram_tensor("h2c", [128, 128], F32, kind="ExternalInput").ap()
    id_d = nc.dram_tensor("identc", [128, 128], F16, kind="ExternalInput").ap()
    y_d = nc.dram_tensor("y", [D, T], F32, kind="ExternalOutput").ap()
    scr_d = nc.dram_tensor("scr", [D, T], F32, kind="Internal").ap()

    A16 = 93 * 1024 + 512
    A32 = 4416

    live_ps = []

    def ps_alloc(name, shape, dt):
        cm = nc.psum_tensor(name, shape, dt)
        t = cm.__enter__()
        live_ps.append((name, cm))
        return t

    def ps_free(*names):
        for name in names:
            nm, cm = live_ps.pop()
            assert nm == name, f"psum free order: {nm} != {name}"
            cm.__exit__(None, None, None)

    from contextlib import ExitStack
    _es = ExitStack()

    def _sem(nm):
        return _es.enter_context(nc.semaphore(nm))

    with _es:
        arena16_t = _es.enter_context(nc.sbuf_tensor("arena16", [128, A16], F16))
        arena32_t = _es.enter_context(nc.sbuf_tensor("arena32", [128, A32], F32))
        hdp_t = _es.enter_context(nc.sbuf_tensor("hdp", [128, 128], F16))
        hdpn_t = _es.enter_context(nc.sbuf_tensor("hdpn", [128, 128], F16))
        ident_t = _es.enter_context(nc.sbuf_tensor("ident", [128, 128], F16))
        vecs_t = _es.enter_context(nc.sbuf_tensor("vecs", [128, NCD * 5 + NCI * 5 + 2], F32))
        stat_t = _es.enter_context(nc.sbuf_tensor("stat", [128, 64], F32))
        s_gp = _sem("s_gp")
        s_w0 = _sem("s_w0")
        s_w1 = _sem("s_w1")
        s_x0 = _sem("s_x0")
        s_x1 = _sem("s_x1")
        s_o0 = _sem("s_o0")
        s_o1 = _sem("s_o1")
        s_pe = _sem("s_pe")
        s_dv = _sem("s_dv")
        s_sc = _sem("s_sc")
        s_gs = _sem("s_gs")
        s_d0 = _sem("s_d0")
        s_d1 = _sem("s_d1")
        s_d2 = _sem("s_d2")
        block = _es.enter_context(nc.Block())
        a16 = Arena(arena16_t[:], A16)
        a32 = Arena(arena32_t[:], A32)
        hdp = hdp_t[:]
        hdpn = hdpn_t[:]
        ident = ident_t[:]
        stat = stat_t[:]
        vecs = {}
        off = 0
        for nm, w in (("bv", NCD), ("bo", NCD), ("suug", NCD), ("wsd", NCD),
                      ("sdpe", NCD), ("sudp", NCI), ("wsu", NCI), ("wsg", NCI),
                      ("supe", NCI), ("sgpe", NCI), ("bveff", 1), ("pad", 1)):
            vecs[nm] = vecs_t[:, off:off + w]
            off += w
        gp = Ctr(s_gp)
        sw = [Ctr(s_w0), Ctr(s_w1)]
        sx = [Ctr(s_x0), Ctr(s_x1)]
        so = [Ctr(s_o0), Ctr(s_o1)]
        pe = Ctr(s_pe)
        dv = Ctr(s_dv)
        sc = Ctr(s_sc)
        gs = Ctr(s_gs)
        sd = [Ctr(s_d0), Ctr(s_d1), Ctr(s_d2)]

        hsu = a16.alloc("hsu", NCD * 128)
        h1sb = a32.alloc("h1sb", 128)
        h2sb = a32.alloc("h2sb", 128)
        tmp64 = a32.alloc("tmp64", NCI)

        @block.gpsimd
        def _(g):
            for dst, src in ((ident, id_d), (h1sb, h1_d), (h2sb, h2_d)):
                gp.inc(g.dma_start(out=dst, in_=src[:]), 16)
            for nm in ("bv", "bo", "suug", "sudp", "wsu", "wsg", "wsd"):
                gp.inc(g.dma_start(out=vecs[nm], in_=vec_d[nm][:]), 16)
        const_dma = gp.n

        @block.vector
        def _(v):
            v.wait_ge(s_gp, const_dma)
            for c in range(NCD):
                dv.inc(v.tensor_scalar(hsu[:, c * 128:(c + 1) * 128], h1sb,
                                       vecs["suug"][:, c:c + 1], None, ALU.mult))
            dv.inc(v.tensor_copy(hdp, h2sb))
            dv.inc(v.tensor_scalar(hdpn, h2sb, -1.0, None, ALU.mult))
            dv.inc(v.tensor_scalar(tmp64, vecs["wsu"], FWHT_K1, None, ALU.mult))
            dv.inc(v.reduce_sum(vecs["bveff"], vecs["bv"], axis=AX))
            v.wait_ge(s_dv, dv.n)
            dv.inc(v.tensor_mul(vecs["supe"], tmp64, vecs["sudp"]))
            dv.inc(v.tensor_scalar(vecs["sgpe"], vecs["wsg"], FWHT_K1, None, ALU.mult))
            dv.inc(v.tensor_scalar(vecs["sdpe"], vecs["wsd"], FWHT_K2, None, ALU.mult))
            dv.inc(v.memset(vecs["pad"], EPS))
        const_dv = dv.n

        xn = a16.alloc("xn", NCD * T)
        xn_tm_b = [a16.alloc(f"xn_tm{i}", D) for i in range(2)]
        xt_b = [a32.alloc(f"xt{i}", D) for i in range(2)]
        veff = a16.alloc("veff", NCD * 128)
        woeff = [a16.alloc(f"woe{i}", D) for i in range(2)]
        vwsl = [a16.alloc(f"vwsl{i}", 2 * D) for i in range(2)]
        vtmp = a16.alloc("vtmp", 3584)
        pstr = [ps_alloc(f"pstr{i}", [128, 1024], F16) for i in range(2)]

        NPAIR = NCD // 2
        NFP = max(1, NPAIR // NT)
        assert NFP * NT == NPAIR or NT == 1
        ln_sc_done = 3 * NT
        ln_pe_done = 2 * NT

        sqrt_sc = [0] * NT
        evac1_sc = [0] * NT
        _scpos = 0
        for _t in range(NT):
            _scpos += 1
            sqrt_sc[_t] = _scpos
            if _t >= 1:
                _scpos += 2
                evac1_sc[_t - 1] = _scpos
        _scpos += 2
        evac1_sc[NT - 1] = _scpos

        eps_mark = [0] * NT
        apply_mark = [0] * NT
        _dvpos = const_dv
        for _t in range(NT + 1):
            if _t < NT:
                _dvpos += 5
                eps_mark[_t] = _dvpos
            if _t >= 1:
                _dvpos += 3
                apply_mark[_t - 1] = _dvpos
        ln_dv_done = _dvpos

        wo_final = woeff[(NPAIR * 2 - 1) % 2]
        vw_mark = []

        @block.gpsimd
        def _(g):
            for p in range(NPAIR):
                if p >= 1:
                    g.wait_ge(s_gs, 6 * p)
                for half in range(2):
                    k = 2 * p + half
                    sw[0].inc(g.dma_start(out=vwsl[0][:, half * D:(half + 1) * D],
                                          in_=wv_d[k * 128:(k + 1) * 128, :]), 16)
                    sw[1].inc(g.dma_start(out=vwsl[1][:, half * D:(half + 1) * D],
                                          in_=wo_d[k * 128:(k + 1) * 128, :]), 16)
                vw_mark.append((sw[0].n, sw[1].n))
                g.wait_ge(s_w0, sw[0].n)
                v0 = vwsl[0].rearrange("p (b a two d) -> p b a two d",
                                       b=2, two=2, d=128)
                vt1 = vtmp[:, 0:2048].rearrange("p (b a d) -> p b a d", b=2, d=128)
                gs.inc(g.tensor_add(vt1, v0[:, :, :, 0, :], v0[:, :, :, 1, :]))
                g.wait_ge(s_gs, gs.n)
                t1f = vtmp[:, 0:2048].rearrange("p (b a two d) -> p b a two d",
                                                b=2, two=2, d=128)
                vt2 = vtmp[:, 2048:3072].rearrange("p (b a d) -> p b a d", b=2, d=128)
                gs.inc(g.tensor_add(vt2, t1f[:, :, :, 0, :], t1f[:, :, :, 1, :]))
                g.wait_ge(s_gs, gs.n)
                t2f = vtmp[:, 2048:3072].rearrange("p (b a two d) -> p b a two d",
                                                   b=2, two=2, d=128)
                vt3 = vtmp[:, 3072:3584].rearrange("p (b a d) -> p b a d", b=2, d=128)
                gs.inc(g.tensor_add(vt3, t2f[:, :, :, 0, :], t2f[:, :, :, 1, :]))
                g.wait_ge(s_gs, gs.n)
                t3f = vtmp[:, 3072:3584].rearrange("p (b two d) -> p b two d",
                                                   b=2, d=128)
                gs.inc(g.tensor_add(
                    veff[:, 2 * p * 128:(2 * p + 2) * 128].rearrange(
                        "p (b one d) -> p b one d", one=1, d=128),
                    t3f[:, :, 0:1, :], t3f[:, :, 1:2, :]))
                g.wait_ge(s_w1, vw_mark[p][1])
                for half in range(2):
                    k = 2 * p + half
                    if k == 0:
                        gs.inc(g.tensor_copy(woeff[0], vwsl[1][:, 0:D]))
                    else:
                        g.wait_ge(s_gs, gs.n)
                        gs.inc(g.tensor_add(woeff[k % 2], woeff[(k - 1) % 2],
                                            vwsl[1][:, half * D:(half + 1) * D]))
        fold_done = gs.n
        assert fold_done == 6 * NPAIR

        xt_mark = []

        @block.sync
        def _(s):
            for t in range(NT):
                if t >= 2:
                    s.wait_ge(s_dv, apply_mark[t - 2])
                sx[t % 2].inc(s.dma_start(out=xt_b[t % 2],
                                          in_=x_tm_d[t * 128:(t + 1) * 128, :]), 16)
                xt_mark.append((sx[t % 2].h, sx[t % 2].n))

        @block.scalar
        def _(s):
            def evac1(t):
                s.wait_ge(s_pe, 2 * (t + 1))
                dst0 = xn.rearrange("p (c t2) -> p c t2", c=NCD)[
                    :, 0:8, t * 128:(t + 1) * 128]
                sc.inc(s.activation(dst0, pstr[0][:].rearrange(
                    "p (c t2) -> p c t2", c=8), AF.Identity))
                dst = xn.rearrange("p (c t2) -> p c t2", c=NCD)[
                    :, 8:16, t * 128:(t + 1) * 128]
                sc.inc(s.activation(dst, pstr[1][:].rearrange(
                    "p (c t2) -> p c t2", c=8), AF.Identity))
            for t in range(NT):
                s.wait_ge(s_dv, eps_mark[t])
                sc.inc(s.activation(stat[:, 5 * t + 4:5 * t + 5],
                                    stat[:, 5 * t + 1:5 * t + 2], AF.Sqrt, bias=vecs["pad"]))
                if t >= 1:
                    evac1(t - 1)
            evac1(NT - 1)
        assert sc.n == ln_sc_done

        @block.vector
        def _(v):
            NB = D // 512
            for t in range(NT + 1):
                if t < NT:
                    v.wait_ge(*xt_mark[t])
                    v.wait_ge(s_dv, dv.n)
                    for q in range(NB):
                        dv.inc(v.bn_stats(stat[:, 40 + q * 6:40 + (q + 1) * 6],
                                          xt_b[t % 2][:, q * 512:(q + 1) * 512]))
                    v.wait_ge(s_dv, dv.n)
                    dv.inc(v.bn_aggr(stat[:, 5 * t:5 * t + 2], stat[:, 40:40 + 6 * NB]))
                    assert dv.n == eps_mark[t]
                if t >= 1:
                    tt = t - 1
                    v.wait_ge(s_sc, sqrt_sc[tt])
                    dv.inc(v.reciprocal(stat[:, 5 * tt + 2:5 * tt + 3],
                                        stat[:, 5 * tt + 4:5 * tt + 5]))
                    v.wait_ge(s_dv, dv.n)
                    dv.inc(v.scalar_tensor_tensor(stat[:, 5 * tt + 3:5 * tt + 4],
                                                  stat[:, 5 * tt:5 * tt + 1], -1.0,
                                                  stat[:, 5 * tt + 2:5 * tt + 3],
                                                  ALU.mult, ALU.mult))
                    v.wait_ge(s_dv, dv.n)
                    dv.inc(v.tensor_scalar(xn_tm_b[tt % 2], xt_b[tt % 2],
                                           stat[:, 5 * tt + 2:5 * tt + 3],
                                           stat[:, 5 * tt + 3:5 * tt + 4],
                                           ALU.mult, ALU.add))
                    assert dv.n == apply_mark[tt]
        assert dv.n == ln_dv_done

        @block.tensor
        def _(te):
            for t in range(NT):
                te.wait_ge(s_dv, apply_mark[t])
                if t >= 1:
                    te.wait_ge(s_sc, evac1_sc[t - 1])
                for gb in range(2):
                    for c in range(8):
                        cc = gb * 8 + c
                        ins = te.transpose(pstr[gb][:, c * 128:(c + 1) * 128],
                                           xn_tm_b[t % 2][:, cc * 128:(cc + 1) * 128],
                                           ident)
                    pe.inc(ins)
        assert pe.n == ln_pe_done and dv.n == ln_dv_done and sc.n == ln_sc_done

        vsum = a16.alloc("vsum", T)
        a16.release("xn_tm0", "xn_tm1")
        a32.release("xt0", "xt1", "h1sb", "h2sb", "tmp64")
        ps_free("pstr1", "pstr0")
        psV = ps_alloc("psV", [128, T], F32)
        psO = [ps_alloc(f"psO{i}", [128, T], F32) for i in range(2)]
        psQx = ps_alloc("psQx", [128, T], F32)
        xfb = [a32.alloc(f"xfb{i}", T) for i in range(2)]
        accb = [a32.alloc(f"acb{i}", T) for i in range(2)]
        a16.release("vwsl0", "vwsl1", "vtmp")
        x1 = [a16.alloc(f"x1_{i}", T, high=True) for i in range(NCD)]
        x1 = x1 + [a16.alloc(f"sp{i}", T, high=True) for i in range(2)]

        vs_pe_done = ln_pe_done + 1
        vs_dv_done = ln_dv_done + 1
        f1_pe_done = vs_pe_done + NCD
        wo_pe_done = f1_pe_done + NCD
        f1_ev = []
        _dvx, _scx = vs_dv_done, ln_sc_done
        for c in range(NCD):
            if c % 2 == 0:
                _dvx += 1
            else:
                _scx += 1
            f1_ev.append((s_dv if c % 2 == 0 else s_sc, _dvx if c % 2 == 0 else _scx))
        f1_dv_done, f1_sc_done = _dvx, _scx

        def wo_acc_mark(m):
            return f1_dv_done + m + 1

        wo_dv_done = f1_dv_done + NCD
        f1b_dv_done = wo_dv_done + 4 * NCD

        @block.tensor
        def _(te):
            te.wait_ge(s_dv, ln_dv_done)
            te.wait_ge(s_sc, ln_sc_done)
            te.wait_ge(s_gs, fold_done)
            for hf in range(NH):
                for k in range(NCD):
                    ins = te.matmul(psV[:, hf * HW:(hf + 1) * HW],
                                    veff[:, k * 128:(k + 1) * 128],
                                    xn[:, k * T + hf * HW: k * T + (hf + 1) * HW],
                                    start=(k == 0), stop=(k == NCD - 1))
            pe.inc(ins)
            for c in range(NCD):
                if c == 1:
                    te.wait_ge(s_dv, vs_dv_done)
                if c >= 2:
                    te.wait_ge(f1_ev[c - 2][0], f1_ev[c - 2][1])
                dstp = psQx if c % 2 == 0 else psV
                for hf in range(NH):
                    ins = te.matmul(dstp[:, hf * HW:(hf + 1) * HW],
                                    hsu[:, c * 128:(c + 1) * 128],
                                    xn[:, c * T + hf * HW: c * T + (hf + 1) * HW],
                                    start=True, stop=True)
                pe.inc(ins)
            te.wait_ge(s_dv, vs_dv_done)
            for m in range(NCD):
                par = m % 2
                if m >= 2:
                    te.wait_ge(s_dv, wo_acc_mark(m - 2))
                for hf in range(NH):
                    ins = te.matmul(psO[par][:, hf * HW:(hf + 1) * HW],
                                    wo_final[:, m * 128:(m + 1) * 128],
                                    vsum[:, hf * HW:(hf + 1) * HW],
                                    start=True, stop=True)
                pe.inc(ins)
        assert pe.n == wo_pe_done

        @block.scalar
        def _(s):
            for c in range(1, NCD, 2):
                s.wait_ge(s_pe, vs_pe_done + c + 1)
                sc.inc(s.activation(x1[c], psV[:], AF.Identity))
        assert sc.n == f1_sc_done

        x1_slots = list(range(NCD))
        x1_final_spare = []

        wo_xf_dma = []
        wo_st_mark = []

        @block.sync
        def _(s):
            s.wait_ge(s_dv, vs_dv_done)
            s.wait_ge(s_sc, ln_sc_done)
            for m in range(NCD):
                if m >= 2:
                    s.wait_ge(s_dv, wo_acc_mark(m - 2))
                sx[m % 2].inc(s.dma_start(out=xfb[m % 2],
                                          in_=x_fm_d[m * 128:(m + 1) * 128, :]), 16)
                wo_xf_dma.append((sx[m % 2].h, sx[m % 2].n))
                s.wait_ge(s_dv, wo_acc_mark(m))
                so[m % 2].inc(s.dma_start(out=scr_d[m * 128:(m + 1) * 128, :],
                                          in_=accb[m % 2]), 16)
                wo_st_mark.append((so[m % 2].h, so[m % 2].n))
        wo_so_done = (so[0].n, so[1].n)

        @block.vector
        def _(v):
            v.wait_ge(s_pe, vs_pe_done)
            dv.inc(v.tensor_scalar(vsum, psV[:], vecs["bveff"], None, ALU.add))
            for c in range(0, NCD, 2):
                v.wait_ge(s_pe, vs_pe_done + c + 1)
                dv.inc(v.tensor_copy(x1[c], psQx[:]))
            assert dv.n == f1_dv_done
            for m in range(NCD):
                par = m % 2
                v.wait_ge(s_pe, f1_pe_done + m + 1)
                v.wait_ge(*wo_xf_dma[m])
                if m >= 2:
                    v.wait_ge(*wo_st_mark[m - 2])
                dv.inc(v.scalar_tensor_tensor(accb[par],
                                              psO[par][:],
                                              vecs["bo"][:, m:m + 1],
                                              xfb[par], ALU.add, ALU.add))
            assert dv.n == wo_dv_done
            v.wait_ge(s_pe, f1_pe_done)
            v.wait_ge(s_sc, f1_sc_done)
            spare = [NCD, NCD + 1]
            for st in range(4):
                step = 1 << st
                for c in range(NCD):
                    if c & step:
                        continue
                    a, b = x1_slots[c], x1_slots[c | step]
                    fa, fb = spare
                    v.wait_ge(s_dv, dv.n)
                    dv.inc(v.tensor_add(x1[fa], x1[a], x1[b]))
                    dv.inc(v.tensor_sub(x1[fb], x1[a], x1[b]))
                    spare = [a, b]
                    x1_slots[c], x1_slots[c | step] = fa, fb
            x1_final_spare.extend(spare)
            assert dv.n == f1b_dv_done


        x1_free_names = [(f"x1_{ix}" if ix < NCD else f"sp{ix - NCD}")
                         for ix in x1_final_spare]

        a16.release("xn", "hsu", "veff", "woe0", "woe1", "vsum", *x1_free_names)
        hbuf = [a16.alloc(f"h_{i}", T, high=True) for i in range(NCI)]
        hbuf = hbuf + [a16.alloc(f"hsp{i}", T, high=True) for i in range(2)]
        uslab = [a16.alloc(f"usl{i}", NCD * 128) for i in range(2)]
        gslab = [a16.alloc(f"gsl{i}", NCD * 128) for i in range(2)]
        sgb = [a16.alloc(f"sgb{i}", T, high=True) for i in range(2)]
        ps_free("psQx", "psO1", "psO0", "psV")
        psU = [ps_alloc(f"psU{i}", [128, 2 * T], F32) for i in range(2)]

        ug_slab_dma = []

        bfly_sched = {i: [] for i in range(NCI)}
        for s5 in range(5):
            step5 = 1 << s5
            for c5 in range(NCI):
                if c5 & step5:
                    continue
                bfly_sched[c5 | (2 * step5 - 1)].append((s5, c5))
        hmul_mark = []
        hmul_only = []
        _m = f1b_dv_done
        for i in range(NCI):
            _m += 1
            hmul_only.append(_m)
            _m += 2 * len(bfly_sched[i])
            hmul_mark.append(_m)

        def ug_silu(i):
            return f1_sc_done + i + 1

        upg_base = wo_pe_done

        ug_dv_done = hmul_mark[-1]
        ug_sc_done = f1_sc_done + NCI
        ug_pe_done = wo_pe_done + NCI

        @block.gpsimd
        def _(g):
            g.wait_ge(s_pe, wo_pe_done)
            g.wait_ge(s_dv, f1b_dv_done)
            for i in range(NCI):
                if i >= 2:
                    g.wait_ge(s_pe, wo_pe_done + i - 1)
                sw[i % 2].inc(g.dma_start(out=uslab[i % 2],
                                          in_=wup_d[i * 128:(i + 1) * 128, :]), 16)
                sw[i % 2].inc(g.dma_start(out=gslab[i % 2],
                                          in_=wgt_d[i * 128:(i + 1) * 128, :]), 16)
                ug_slab_dma.append((sw[i % 2].h, sw[i % 2].n))

        @block.tensor
        def _(te):
            te.wait_ge(s_dv, f1b_dv_done)
            te.wait_ge(s_sc, f1_sc_done)
            for i in range(NCI):
                par = i % 2
                te.wait_ge(*ug_slab_dma[i])
                if i >= 2:
                    te.wait_ge(s_sc, ug_silu(i - 2))
                    te.wait_ge(s_dv, hmul_only[i - 2])
                for hf in range(NH):
                    for k in range(NCD):
                        te.matmul(psU[par][:, hf * HW:(hf + 1) * HW],
                                  uslab[par][:, k * 128:(k + 1) * 128],
                                  x1[x1_slots[k]][:, hf * HW:(hf + 1) * HW],
                                  start=(k == 0), stop=(k == NCD - 1))
                for hf in range(NH):
                    for k in range(NCD):
                        ins = te.matmul(psU[par][:, T + hf * HW: T + (hf + 1) * HW],
                                        gslab[par][:, k * 128:(k + 1) * 128],
                                        x1[x1_slots[k]][:, hf * HW:(hf + 1) * HW],
                                        start=(k == 0), stop=(k == NCD - 1))
                pe.inc(ins)

        @block.scalar
        def _(s):
            for i in range(NCI):
                par = i % 2
                s.wait_ge(s_pe, upg_base + i + 1)
                if i >= 2:
                    s.wait_ge(s_dv, hmul_only[i - 2])
                sc.inc(s.activation(sgb[par], psU[par][:, T:2 * T], AF.Silu,
                                    scale=vecs["sgpe"][:, i:i + 1]))

        h_slots = list(range(NCI))
        h_level = [0] * NCI
        cross_order = []

        @block.vector
        def _(v):
            spare = [NCI, NCI + 1]
            for i in range(NCI):
                par = i % 2
                v.wait_ge(s_pe, upg_base + i + 1)
                v.wait_ge(s_sc, ug_silu(i))
                dv.inc(v.scalar_tensor_tensor(hbuf[i], psU[par][:, 0:T],
                                              vecs["supe"][:, i:i + 1],
                                              sgb[par], ALU.mult, ALU.mult))
                for (s5, c5) in bfly_sched[i]:
                    d5 = c5 | (1 << s5)
                    a, b = h_slots[c5], h_slots[d5]
                    fa, fb = spare
                    v.wait_ge(s_dv, dv.n)
                    dv.inc(v.tensor_add(hbuf[fa], hbuf[a], hbuf[b]))
                    dv.inc(v.tensor_sub(hbuf[fb], hbuf[a], hbuf[b]))
                    spare = [a, b]
                    h_slots[c5], h_slots[d5] = fa, fb
                    h_level[c5] += 1
                    h_level[d5] += 1
                    for cc in (c5, d5):
                        if h_level[cc] == 5:
                            cross_order.append((cc, dv.n))
                assert dv.n == hmul_mark[i]
        assert dv.n == ug_dv_done and sc.n == ug_sc_done and pe.n == ug_pe_done
        assert len(cross_order) == NCI and all(l == 5 for l in h_level)

        for nm6 in ([f"x1_{i2}" for i2 in range(NCD)] + ["sp0", "sp1"]):
            if nm6 not in x1_free_names:
                a16.release(nm6)

        f2_ev = []
        _dvx, _scx = ug_dv_done, ug_sc_done
        for c in range(NCI):
            if c % 2 == 0:
                _dvx += 1
            else:
                _scx += 1
            f2_ev.append((_dvx, _scx))
        f2_dv_done, f2_sc_done = _dvx, _scx
        f2_pe_done = ug_pe_done + NCI

        F2DEPTH = 4 if T >= 512 else 2

        def psq4(j):
            if F2DEPTH == 4:
                return psU[j % 2][:, ((j // 2) % 2) * T:(((j // 2) % 2) + 1) * T]
            return psU[j % 2][:, 0:T]

        cmark = {c: mk for c, mk in cross_order}
        pairs = sorted(range(NCI // 2), key=lambda p: max(cmark[p], cmark[p + 32]))
        out_chunks = []
        for p in pairs:
            out_chunks.append((p, max(cmark[p], cmark[p + 32]), +1))
            out_chunks.append((p + 32, max(cmark[p], cmark[p + 32]), -1))

        @block.tensor
        def _(te):
            te.wait_ge(s_sc, ug_sc_done)
            for j in range(NCI):
                c, mk, sgn = out_chunks[j]
                p = c % 32
                te.wait_ge(s_dv, mk)
                if j >= F2DEPTH:
                    te.wait_ge(s_dv, f2_ev[j - F2DEPTH][0])
                    te.wait_ge(s_sc, f2_ev[j - F2DEPTH][1])
                dst = psq4(j)
                for hf in range(NH):
                    te.matmul(dst[:, hf * HW:(hf + 1) * HW],
                              hdp,
                              hbuf[h_slots[p]][:, hf * HW:(hf + 1) * HW],
                              start=True, stop=False)
                    ins = te.matmul(dst[:, hf * HW:(hf + 1) * HW],
                                    hdp if sgn > 0 else hdpn,
                                    hbuf[h_slots[p + 32]][:, hf * HW:(hf + 1) * HW],
                                    start=False, stop=True)
                pe.inc(ins)

        @block.vector
        def _(v):
            for j in range(0, NCI, 2):
                v.wait_ge(s_pe, ug_pe_done + (j | 1) + 1)
                dv.inc(v.tensor_copy(hbuf[h_slots[out_chunks[j][0]]], psq4(j)))

        @block.scalar
        def _(s):
            for j in range(1, NCI, 2):
                s.wait_ge(s_pe, ug_pe_done + j + 1)
                sc.inc(s.activation(hbuf[h_slots[out_chunks[j][0]]], psq4(j), AF.Identity))
        assert dv.n == f2_dv_done and sc.n == f2_sc_done and pe.n == f2_pe_done
        f2b_dv_done = f2_dv_done

        a16.release("usl0", "usl1", "gsl0", "gsl1", "sgb0", "sgb1")
        a32.release("xfb0", "xfb1", "acb0", "acb1")
        dslab = [a16.alloc(f"dsl{i}", 16 * 128) for i in range(12)]
        xfb2 = [a32.alloc(f"xf2{i}", T) for i in range(2)]
        outb = [a32.alloc(f"out{i}", T) for i in range(2)]

        dn_slab_dma = []
        dn_acc_dma = []
        dn_st_mark = []

        def dn_fin(m):
            return f2b_dv_done + m + 1

        @block.gpsimd
        def _(g):
            g.wait_ge(s_dv, ug_dv_done)
            g.wait_ge(s_pe, ug_pe_done)
            for m in range(NCD):
                if m >= 3:
                    g.wait_ge(s_pe, f2_pe_done + m - 2)
                for kq in range(4):
                    idx = m * 4 + kq
                    sd[m % 3].inc(g.dma_start(
                        out=dslab[(m % 3) * 4 + kq],
                        in_=wdn_d[idx * 128:(idx + 1) * 128, :]), 16)
                dn_slab_dma.append((sd[m % 3].h, sd[m % 3].n))

        @block.sync
        def _(s):
            s.wait_ge(s_dv, f2b_dv_done)
            s.wait_ge(s_sc, f2_sc_done)
            for m in range(NCD):
                if m >= 2:
                    s.wait_ge(s_dv, dn_fin(m - 2))
                sx[m % 2].inc(s.dma_start(out=xfb2[m % 2],
                                          in_=scr_d[m * 128:(m + 1) * 128, :]), 16)
                dn_acc_dma.append((sx[m % 2].h, sx[m % 2].n))
                s.wait_ge(s_dv, dn_fin(m))
                so[m % 2].inc(s.dma_start(out=y_d[m * 128:(m + 1) * 128, :],
                                          in_=outb[m % 2]), 16)
                dn_st_mark.append((so[m % 2].h, so[m % 2].n))
            s.wait_ge(s_o0, so[0].n)
            s.wait_ge(s_o1, so[1].n)

        @block.tensor
        def _(te):
            te.wait_ge(s_dv, f2b_dv_done)
            te.wait_ge(s_sc, f2_sc_done)
            for m in range(NCD):
                par = m % 2
                if m >= 2:
                    te.wait_ge(s_dv, dn_fin(m - 2))
                te.wait_ge(*dn_slab_dma[m])
                for hf in range(NH):
                    for kq in range(4):
                        for k in range(16):
                            kk = kq * 16 + k
                            ins = te.matmul(
                                psU[par][:, hf * HW:(hf + 1) * HW],
                                dslab[(m % 3) * 4 + kq][:, k * 128:(k + 1) * 128],
                                hbuf[h_slots[kk]][:, hf * HW:(hf + 1) * HW],
                                start=(kq == 0 and k == 0), stop=(kq == 3 and k == 15))
                pe.inc(ins)

        @block.vector
        def _(v):
            v.wait_ge(s_o0, wo_so_done[0])
            v.wait_ge(s_o1, wo_so_done[1])
            for m in range(NCD):
                par = m % 2
                v.wait_ge(s_pe, f2_pe_done + m + 1)
                v.wait_ge(*dn_acc_dma[m])
                if m >= 2:
                    v.wait_ge(*dn_st_mark[m - 2])
                dv.inc(v.scalar_tensor_tensor(outb[m % 2],
                                              psU[par][:, 0:T],
                                              vecs["sdpe"][:, m:m + 1],
                                              xfb2[m % 2], ALU.mult, ALU.add))
        ps_free("psU1", "psU0")

    return nc


def _tile_upgate(w):
    wt = np.ascontiguousarray(w.T)
    t = wt.reshape(16, 128, 64, 128).transpose(2, 1, 0, 3)
    return np.ascontiguousarray(t).reshape(64 * 128, 16 * 128)


def _tile_down(w):
    wt = np.ascontiguousarray(w.T)
    t = wt.reshape(4, 16, 128, 16, 128).transpose(3, 0, 2, 1, 4)
    return np.ascontiguousarray(t).reshape(64 * 128, 16 * 128)


def _hadamard(n):
    h = np.array([[1.0]], dtype=np.float64)
    while h.shape[0] < n:
        h = np.block([[h, h], [h, -h]])
    return h


def _vec(v, nch):
    return np.ascontiguousarray(np.asarray(v, np.float32).reshape(nch, 128).T)


def kernel(L0_attn_x, L0_attn_Wq, L0_attn_bq, L0_attn_Wk, L0_attn_bk,
           L0_attn_Wv, L0_attn_bv, L0_attn_Wo, L0_attn_bo,
           L0_ffn_SU_ug, L0_ffn_SU_dp, L0_ffn_Wscale_ug, L0_ffn_Wscale_dp,
           L0_ffn_W_up, L0_ffn_W_gate, L0_ffn_W_down):
    x = np.ascontiguousarray(np.asarray(L0_attn_x, dtype=np.float32)).reshape(-1, D)
    n_tok = x.shape[0]
    T = n_tok // N_CORES

    ws = np.asarray(L0_ffn_Wscale_ug, np.float32)
    shared = dict(
        wv=np.ascontiguousarray(np.asarray(L0_attn_Wv, np.float32).T),
        wo=np.ascontiguousarray(np.asarray(L0_attn_Wo, np.float32).T),
        wup=_tile_upgate(np.asarray(L0_ffn_W_up, np.float32)),
        wgt=_tile_upgate(np.asarray(L0_ffn_W_gate, np.float32)),
        wdn=_tile_down(np.asarray(L0_ffn_W_down, np.float32)),
        bv=_vec(L0_attn_bv, NCD), bo=_vec(L0_attn_bo, NCD),
        suug=_vec(L0_ffn_SU_ug, NCD), sudp=_vec(L0_ffn_SU_dp, NCI),
        wsu=_vec(ws[:I], NCI), wsg=_vec(ws[I:], NCI),
        wsd=_vec(L0_ffn_Wscale_dp, NCD),
        h1c=(_hadamard(128) * FWHT_A).astype(np.float32),
        h2c=(_hadamard(128) * FWHT_A).astype(np.float32),
        identc=np.eye(128, dtype=np.float16),
    )

    nc = build_program(T)
    in_maps = []
    for ci in range(N_CORES):
        xs = x[ci * T:(ci + 1) * T]
        m = dict(shared)
        m["x_tm"] = np.ascontiguousarray(xs)
        m["x_fm"] = np.ascontiguousarray(xs.T)
        in_maps.append(m)

    res = run_bass_kernel_spmd(nc, in_maps, core_ids=list(range(N_CORES)))
    y = np.concatenate([res.results[ci]["y"] for ci in range(N_CORES)], axis=1)
    return np.ascontiguousarray(y.T).reshape(np.asarray(L0_attn_x).shape).astype(np.float32)


# version 21
# speedup vs baseline: 1.0343x; 1.0343x over previous
import numpy as np

import concourse.bass as bass
import concourse.mybir as mybir
from concourse.bass_utils import run_bass_kernel_spmd

F32 = mybir.dt.float32
F16 = mybir.dt.float16
AF = mybir.ActivationFunctionType
ALU = mybir.AluOpType
AX = mybir.AxisListType.X

D = 2048
H = 16
I = 8192
NCD = D // 128
NCI = I // 128
N_CORES = 8
EPS = 1e-5
FWHT_A = 0.0004873637808486819
FWHT_K1 = 45.34002685546875
FWHT_K2 = 22.670013427734375


class Ctr:
    def __init__(self, handle):
        self.h = handle
        self.n = 0

    def inc(self, ins, k=1):
        ins.then_inc(self.h, k)
        self.n += k
        return self.n


class Arena:

    def __init__(self, ap, ncols):
        self.ap = ap
        self.avail = [(0, ncols)]
        self.used = {}

    def alloc(self, name, n, high=False):
        idxs = range(len(self.avail) - 1, -1, -1) if high else range(len(self.avail))
        for i in idxs:
            s, l = self.avail[i]
            if l >= n:
                if high:
                    start = s + l - n
                    self.used[name] = (start, n)
                    if l == n:
                        self.avail.pop(i)
                    else:
                        self.avail[i] = (s, l - n)
                    return self.ap[:, start:start + n]
                self.used[name] = (s, n)
                if l == n:
                    self.avail.pop(i)
                else:
                    self.avail[i] = (s + n, l - n)
                return self.ap[:, s:s + n]
        raise MemoryError(f"arena: {name} needs {n} cols; free={self.avail}")

    def release(self, *names):
        for name in names:
            s, n = self.used.pop(name)
            self.avail.append((s, n))
        self.avail.sort()
        merged = []
        for s, l in self.avail:
            if merged and merged[-1][0] + merged[-1][1] == s:
                merged[-1] = (merged[-1][0], merged[-1][1] + l)
            else:
                merged.append((s, l))
        self.avail = merged


def build_program(T):
    NT = T // 128
    HW = min(512, T)
    NH = T // HW
    nc = bass.Bass("TRN2", target_bir_lowering=False, debug=False)

    x_tm_d = nc.dram_tensor("x_tm", [T, D], F32, kind="ExternalInput").ap()
    x_fm_d = nc.dram_tensor("x_fm", [D, T], F32, kind="ExternalInput").ap()
    wv_d = nc.dram_tensor("wv", [D, D], F32, kind="ExternalInput").ap()
    wo_d = nc.dram_tensor("wo", [D, D], F32, kind="ExternalInput").ap()
    wup_d = nc.dram_tensor("wup", [NCI * 128, NCD * 128], F32, kind="ExternalInput").ap()
    wgt_d = nc.dram_tensor("wgt", [NCI * 128, NCD * 128], F32, kind="ExternalInput").ap()
    wdn_d = nc.dram_tensor("wdn", [64 * 128, 16 * 128], F32, kind="ExternalInput").ap()
    vec_d = {}
    for nm, w in (("bv", NCD), ("bo", NCD), ("suug", NCD), ("sudp", NCI),
                  ("wsu", NCI), ("wsg", NCI), ("wsd", NCD)):
        vec_d[nm] = nc.dram_tensor(nm, [128, w], F32, kind="ExternalInput").ap()
    h1_d = nc.dram_tensor("h1c", [128, 128], F32, kind="ExternalInput").ap()
    h2_d = nc.dram_tensor("h2c", [128, 128], F32, kind="ExternalInput").ap()
    id_d = nc.dram_tensor("identc", [128, 128], F16, kind="ExternalInput").ap()
    y_d = nc.dram_tensor("y", [D, T], F32, kind="ExternalOutput").ap()
    scr_d = nc.dram_tensor("scr", [D, T], F32, kind="Internal").ap()

    A16 = 93 * 1024 + 512
    A32 = 4416

    live_ps = []

    def ps_alloc(name, shape, dt):
        cm = nc.psum_tensor(name, shape, dt)
        t = cm.__enter__()
        live_ps.append((name, cm))
        return t

    def ps_free(*names):
        for name in names:
            nm, cm = live_ps.pop()
            assert nm == name, f"psum free order: {nm} != {name}"
            cm.__exit__(None, None, None)

    from contextlib import ExitStack
    _es = ExitStack()

    def _sem(nm):
        return _es.enter_context(nc.semaphore(nm))

    with _es:
        arena16_t = _es.enter_context(nc.sbuf_tensor("arena16", [128, A16], F16))
        arena32_t = _es.enter_context(nc.sbuf_tensor("arena32", [128, A32], F32))
        hdp_t = _es.enter_context(nc.sbuf_tensor("hdp", [128, 128], F16))
        hdpn_t = _es.enter_context(nc.sbuf_tensor("hdpn", [128, 128], F16))
        ident_t = _es.enter_context(nc.sbuf_tensor("ident", [128, 128], F16))
        vecs_t = _es.enter_context(nc.sbuf_tensor("vecs", [128, NCD * 5 + NCI * 5 + 2], F32))
        stat_t = _es.enter_context(nc.sbuf_tensor("stat", [128, 64], F32))
        s_gp = _sem("s_gp")
        s_w0 = _sem("s_w0")
        s_w1 = _sem("s_w1")
        s_x0 = _sem("s_x0")
        s_x1 = _sem("s_x1")
        s_o0 = _sem("s_o0")
        s_o1 = _sem("s_o1")
        s_pe = _sem("s_pe")
        s_dv = _sem("s_dv")
        s_sc = _sem("s_sc")
        s_gs = _sem("s_gs")
        s_d0 = _sem("s_d0")
        s_d1 = _sem("s_d1")
        s_d2 = _sem("s_d2")
        block = _es.enter_context(nc.Block())
        a16 = Arena(arena16_t[:], A16)
        a32 = Arena(arena32_t[:], A32)
        hdp = hdp_t[:]
        hdpn = hdpn_t[:]
        ident = ident_t[:]
        stat = stat_t[:]
        vecs = {}
        off = 0
        for nm, w in (("bv", NCD), ("bo", NCD), ("suug", NCD), ("wsd", NCD),
                      ("sdpe", NCD), ("sudp", NCI), ("wsu", NCI), ("wsg", NCI),
                      ("supe", NCI), ("sgpe", NCI), ("bveff", 1), ("pad", 1)):
            vecs[nm] = vecs_t[:, off:off + w]
            off += w
        gp = Ctr(s_gp)
        sw = [Ctr(s_w0), Ctr(s_w1)]
        sx = [Ctr(s_x0), Ctr(s_x1)]
        so = [Ctr(s_o0), Ctr(s_o1)]
        pe = Ctr(s_pe)
        dv = Ctr(s_dv)
        sc = Ctr(s_sc)
        gs = Ctr(s_gs)
        sd = [Ctr(s_d0), Ctr(s_d1), Ctr(s_d2)]

        hsu = a16.alloc("hsu", NCD * 128)
        h1sb = a32.alloc("h1sb", 128)
        h2sb = a32.alloc("h2sb", 128)
        tmp64 = a32.alloc("tmp64", NCI)

        @block.gpsimd
        def _(g):
            for dst, src in ((ident, id_d), (h1sb, h1_d), (h2sb, h2_d)):
                gp.inc(g.dma_start(out=dst, in_=src[:]), 16)
            for nm in ("bv", "bo", "suug", "sudp", "wsu", "wsg", "wsd"):
                gp.inc(g.dma_start(out=vecs[nm], in_=vec_d[nm][:]), 16)
        const_dma = gp.n

        @block.vector
        def _(v):
            v.wait_ge(s_gp, const_dma)
            for c in range(NCD):
                dv.inc(v.tensor_scalar(hsu[:, c * 128:(c + 1) * 128], h1sb,
                                       vecs["suug"][:, c:c + 1], None, ALU.mult))
            dv.inc(v.tensor_copy(hdp, h2sb))
            dv.inc(v.tensor_scalar(hdpn, h2sb, -1.0, None, ALU.mult))
            dv.inc(v.tensor_scalar(tmp64, vecs["wsu"], FWHT_K1, None, ALU.mult))
            dv.inc(v.reduce_sum(vecs["bveff"], vecs["bv"], axis=AX))
            v.wait_ge(s_dv, dv.n)
            dv.inc(v.tensor_mul(vecs["supe"], tmp64, vecs["sudp"]))
            dv.inc(v.tensor_scalar(vecs["sgpe"], vecs["wsg"], FWHT_K1, None, ALU.mult))
            dv.inc(v.tensor_scalar(vecs["sdpe"], vecs["wsd"], FWHT_K2, None, ALU.mult))
        const_dv = dv.n

        xn = a16.alloc("xn", NCD * T)
        xn_tm_b = [a16.alloc(f"xn_tm{i}", D) for i in range(2)]
        xt_b = [a32.alloc(f"xt{i}", D) for i in range(2)]
        veff = a16.alloc("veff", NCD * 128)
        woeff = [a16.alloc(f"woe{i}", D) for i in range(2)]
        vwsl = [a16.alloc(f"vwsl{i}", 2 * D) for i in range(2)]
        vtmp = a16.alloc("vtmp", 3584)
        pstr = [ps_alloc(f"pstr{i}", [128, 1024], F16) for i in range(2)]

        NPAIR = NCD // 2
        NFP = max(1, NPAIR // NT)
        assert NFP * NT == NPAIR or NT == 1
        ln_sc_done = 2 * NT
        ln_pe_done = 2 * NT

        sqrt_sc = [0] * NT
        evac1_sc = [0] * NT
        _scpos = 0
        for _t in range(NT):
            _scpos += 1
            sqrt_sc[_t] = _scpos
            if _t >= 1:
                _scpos += 1
                evac1_sc[_t - 1] = _scpos
        _scpos += 1
        evac1_sc[NT - 1] = _scpos

        eps_mark = [0] * NT
        apply_mark = [0] * NT
        woadd_mark = [0] * NPAIR
        _dvpos = const_dv
        for _t in range(NT + 1):
            if _t < NT:
                _dvpos += 6
                eps_mark[_t] = _dvpos
            if _t >= 1:
                _tt = _t - 1
                _dvpos += 3
                apply_mark[_tt] = _dvpos
                _dvpos += 1
                for _jf in range(NFP):
                    _p = _tt * NFP + _jf
                    if _p < NPAIR:
                        _dvpos += 2
                        woadd_mark[_p] = _dvpos
        ln_dv_done = _dvpos

        wo_final = woeff[(NPAIR * 2 - 1) % 2]
        vw_mark = []

        @block.gpsimd
        def _(g):
            for p in range(NPAIR):
                if p >= 1:
                    g.wait_ge(s_gs, 4 * p)
                    g.wait_ge(s_dv, woadd_mark[p - 1])
                for half in range(2):
                    k = 2 * p + half
                    sw[0].inc(g.dma_start(out=vwsl[0][:, half * D:(half + 1) * D],
                                          in_=wv_d[k * 128:(k + 1) * 128, :]), 16)
                    sw[1].inc(g.dma_start(out=vwsl[1][:, half * D:(half + 1) * D],
                                          in_=wo_d[k * 128:(k + 1) * 128, :]), 16)
                vw_mark.append((sw[0].n, sw[1].n))
                g.wait_ge(s_w0, sw[0].n)
                v0 = vwsl[0].rearrange("p (b a two d) -> p b a two d",
                                       b=2, two=2, d=128)
                vt1 = vtmp[:, 0:2048].rearrange("p (b a d) -> p b a d", b=2, d=128)
                gs.inc(g.tensor_add(vt1, v0[:, :, :, 0, :], v0[:, :, :, 1, :]))
                g.wait_ge(s_gs, gs.n)
                t1f = vtmp[:, 0:2048].rearrange("p (b a two d) -> p b a two d",
                                                b=2, two=2, d=128)
                vt2 = vtmp[:, 2048:3072].rearrange("p (b a d) -> p b a d", b=2, d=128)
                gs.inc(g.tensor_add(vt2, t1f[:, :, :, 0, :], t1f[:, :, :, 1, :]))
                g.wait_ge(s_gs, gs.n)
                t2f = vtmp[:, 2048:3072].rearrange("p (b a two d) -> p b a two d",
                                                   b=2, two=2, d=128)
                vt3 = vtmp[:, 3072:3584].rearrange("p (b a d) -> p b a d", b=2, d=128)
                gs.inc(g.tensor_add(vt3, t2f[:, :, :, 0, :], t2f[:, :, :, 1, :]))
                g.wait_ge(s_gs, gs.n)
                t3f = vtmp[:, 3072:3584].rearrange("p (b two d) -> p b two d",
                                                   b=2, d=128)
                gs.inc(g.tensor_add(
                    veff[:, 2 * p * 128:(2 * p + 2) * 128].rearrange(
                        "p (b one d) -> p b one d", one=1, d=128),
                    t3f[:, :, 0:1, :], t3f[:, :, 1:2, :]))
        veff_done = gs.n
        assert veff_done == 4 * NPAIR

        xt_mark = []

        @block.sync
        def _(s):
            for t in range(NT):
                if t >= 2:
                    s.wait_ge(s_dv, apply_mark[t - 2])
                sx[t % 2].inc(s.dma_start(out=xt_b[t % 2],
                                          in_=x_tm_d[t * 128:(t + 1) * 128, :]), 16)
                xt_mark.append((sx[t % 2].h, sx[t % 2].n))

        @block.scalar
        def _(s):
            def evac1(t):
                s.wait_ge(s_pe, 2 * (t + 1))
                dst = xn.rearrange("p (c t2) -> p c t2", c=NCD)[
                    :, 8:16, t * 128:(t + 1) * 128]
                sc.inc(s.activation(dst, pstr[1][:].rearrange(
                    "p (c t2) -> p c t2", c=8), AF.Identity))
            for t in range(NT):
                s.wait_ge(s_dv, eps_mark[t])
                sc.inc(s.activation(stat[:, 5 * t + 4:5 * t + 5],
                                    stat[:, 5 * t + 1:5 * t + 2], AF.Sqrt))
                if t >= 1:
                    evac1(t - 1)
            evac1(NT - 1)
        assert sc.n == ln_sc_done

        @block.vector
        def _(v):
            NB = D // 512
            for t in range(NT + 1):
                if t < NT:
                    v.wait_ge(*xt_mark[t])
                    for q in range(NB):
                        dv.inc(v.bn_stats(stat[:, 40 + q * 6:40 + (q + 1) * 6],
                                          xt_b[t % 2][:, q * 512:(q + 1) * 512]))
                    v.wait_ge(s_dv, dv.n)
                    dv.inc(v.bn_aggr(stat[:, 5 * t:5 * t + 2], stat[:, 40:40 + 6 * NB]))
                    v.wait_ge(s_dv, dv.n)
                    dv.inc(v.tensor_scalar(stat[:, 5 * t + 1:5 * t + 2],
                                           stat[:, 5 * t + 1:5 * t + 2], EPS, None, ALU.add))
                    assert dv.n == eps_mark[t]
                if t >= 1:
                    tt = t - 1
                    v.wait_ge(s_sc, sqrt_sc[tt])
                    dv.inc(v.reciprocal(stat[:, 5 * tt + 2:5 * tt + 3],
                                        stat[:, 5 * tt + 4:5 * tt + 5]))
                    v.wait_ge(s_dv, dv.n)
                    dv.inc(v.scalar_tensor_tensor(stat[:, 5 * tt + 3:5 * tt + 4],
                                                  stat[:, 5 * tt:5 * tt + 1], -1.0,
                                                  stat[:, 5 * tt + 2:5 * tt + 3],
                                                  ALU.mult, ALU.mult))
                    v.wait_ge(s_dv, dv.n)
                    dv.inc(v.tensor_scalar(xn_tm_b[tt % 2], xt_b[tt % 2],
                                           stat[:, 5 * tt + 2:5 * tt + 3],
                                           stat[:, 5 * tt + 3:5 * tt + 4],
                                           ALU.mult, ALU.add))
                    assert dv.n == apply_mark[tt]
                    v.wait_ge(s_pe, 2 * (tt + 1))
                    dst = xn.rearrange("p (c t2) -> p c t2", c=NCD)[
                        :, 0:8, tt * 128:(tt + 1) * 128]
                    dv.inc(v.tensor_copy(dst, pstr[0][:].rearrange(
                        "p (c t2) -> p c t2", c=8)))
                    for jf in range(NFP):
                        p = tt * NFP + jf
                        if p >= NPAIR:
                            continue
                        v.wait_ge(s_w1, vw_mark[p][1])
                        for half in range(2):
                            k = 2 * p + half
                            if k == 0:
                                dv.inc(v.tensor_copy(woeff[0], vwsl[1][:, 0:D]))
                            else:
                                v.wait_ge(s_dv, dv.n)
                                dv.inc(v.tensor_add(woeff[k % 2], woeff[(k - 1) % 2],
                                                    vwsl[1][:, half * D:(half + 1) * D]))
                        assert dv.n == woadd_mark[p]
        assert dv.n == ln_dv_done

        @block.tensor
        def _(te):
            for t in range(NT):
                te.wait_ge(s_dv, apply_mark[t])
                if t >= 1:
                    te.wait_ge(s_sc, evac1_sc[t - 1])
                for gb in range(2):
                    for c in range(8):
                        cc = gb * 8 + c
                        ins = te.transpose(pstr[gb][:, c * 128:(c + 1) * 128],
                                           xn_tm_b[t % 2][:, cc * 128:(cc + 1) * 128],
                                           ident)
                    pe.inc(ins)
        assert pe.n == ln_pe_done and dv.n == ln_dv_done and sc.n == ln_sc_done

        vsum = a16.alloc("vsum", T)
        a16.release("xn_tm0", "xn_tm1")
        a32.release("xt0", "xt1", "h1sb", "h2sb", "tmp64")
        ps_free("pstr1", "pstr0")
        psV = ps_alloc("psV", [128, T], F32)
        psO = [ps_alloc(f"psO{i}", [128, T], F32) for i in range(2)]
        psQx = ps_alloc("psQx", [128, T], F32)
        xfb = [a32.alloc(f"xfb{i}", T) for i in range(2)]
        accb = [a32.alloc(f"acb{i}", T) for i in range(2)]
        a16.release("vwsl0", "vwsl1", "vtmp")
        x1 = [a16.alloc(f"x1_{i}", T, high=True) for i in range(NCD)]
        x1 = x1 + [a16.alloc(f"sp{i}", T, high=True) for i in range(2)]

        vs_pe_done = ln_pe_done + 1
        vs_dv_done = ln_dv_done + 1
        f1_pe_done = vs_pe_done + NCD
        wo_pe_done = f1_pe_done + NCD
        bf1_sched = {c: [] for c in range(NCD)}
        for _s in range(4):
            _st = 1 << _s
            for _c in range(NCD):
                if _c & _st:
                    continue
                bf1_sched[_c | (2 * _st - 1)].append((_s, _c))
        f1_ev = []
        _dvx, _scx = vs_dv_done, ln_sc_done
        for c in range(NCD):
            if c % 2 == 0:
                _dvx += 1
                f1_ev.append((s_dv, _dvx))
            else:
                _scx += 1
                f1_ev.append((s_sc, _scx))
            _dvx += 2 * len(bf1_sched[c])
        f1_dv_done, f1_sc_done = _dvx, _scx
        f1b_dv_done = f1_dv_done

        def wo_acc_mark(m):
            return f1b_dv_done + m + 1

        wo_dv_done = f1b_dv_done + NCD

        @block.tensor
        def _(te):
            te.wait_ge(s_dv, ln_dv_done)
            te.wait_ge(s_sc, ln_sc_done)
            te.wait_ge(s_gs, veff_done)
            for hf in range(NH):
                for k in range(NCD):
                    ins = te.matmul(psV[:, hf * HW:(hf + 1) * HW],
                                    veff[:, k * 128:(k + 1) * 128],
                                    xn[:, k * T + hf * HW: k * T + (hf + 1) * HW],
                                    start=(k == 0), stop=(k == NCD - 1))
            pe.inc(ins)
            for c in range(NCD):
                if c == 1:
                    te.wait_ge(s_dv, vs_dv_done)
                if c >= 2:
                    te.wait_ge(f1_ev[c - 2][0], f1_ev[c - 2][1])
                dstp = psQx if c % 2 == 0 else psV
                for hf in range(NH):
                    ins = te.matmul(dstp[:, hf * HW:(hf + 1) * HW],
                                    hsu[:, c * 128:(c + 1) * 128],
                                    xn[:, c * T + hf * HW: c * T + (hf + 1) * HW],
                                    start=True, stop=True)
                pe.inc(ins)
            te.wait_ge(s_dv, vs_dv_done)
            for m in range(NCD):
                par = m % 2
                if m >= 2:
                    te.wait_ge(s_dv, wo_acc_mark(m - 2))
                for hf in range(NH):
                    ins = te.matmul(psO[par][:, hf * HW:(hf + 1) * HW],
                                    wo_final[:, m * 128:(m + 1) * 128],
                                    vsum[:, hf * HW:(hf + 1) * HW],
                                    start=True, stop=True)
                pe.inc(ins)
        assert pe.n == wo_pe_done

        @block.scalar
        def _(s):
            for c in range(1, NCD, 2):
                s.wait_ge(s_pe, vs_pe_done + c + 1)
                sc.inc(s.activation(x1[c], psV[:], AF.Identity))
        assert sc.n == f1_sc_done

        x1_slots = list(range(NCD))
        x1_final_spare = []

        wo_xf_dma = []
        wo_st_mark = []

        @block.sync
        def _(s):
            s.wait_ge(s_dv, vs_dv_done)
            s.wait_ge(s_sc, ln_sc_done)
            for m in range(NCD):
                if m >= 2:
                    s.wait_ge(s_dv, wo_acc_mark(m - 2))
                sx[m % 2].inc(s.dma_start(out=xfb[m % 2],
                                          in_=x_fm_d[m * 128:(m + 1) * 128, :]), 16)
                wo_xf_dma.append((sx[m % 2].h, sx[m % 2].n))
                s.wait_ge(s_dv, wo_acc_mark(m))
                so[m % 2].inc(s.dma_start(out=scr_d[m * 128:(m + 1) * 128, :],
                                          in_=accb[m % 2]), 16)
                wo_st_mark.append((so[m % 2].h, so[m % 2].n))
        wo_so_done = (so[0].n, so[1].n)

        @block.vector
        def _(v):
            v.wait_ge(s_pe, vs_pe_done)
            dv.inc(v.tensor_scalar(vsum, psV[:], vecs["bveff"], None, ALU.add))
            spare = [NCD, NCD + 1]
            for c in range(NCD):
                if c % 2 == 0:
                    v.wait_ge(s_pe, vs_pe_done + c + 1)
                    dv.inc(v.tensor_copy(x1[c], psQx[:]))
                else:
                    v.wait_ge(s_sc, f1_ev[c][1])
                for (s5, c5) in bf1_sched[c]:
                    d5 = c5 | (1 << s5)
                    a, b = x1_slots[c5], x1_slots[d5]
                    fa, fb = spare
                    v.wait_ge(s_dv, dv.n)
                    dv.inc(v.tensor_add(x1[fa], x1[a], x1[b]))
                    dv.inc(v.tensor_sub(x1[fb], x1[a], x1[b]))
                    spare = [a, b]
                    x1_slots[c5], x1_slots[d5] = fa, fb
            x1_final_spare.extend(spare)
            assert dv.n == f1b_dv_done
            for m in range(NCD):
                par = m % 2
                v.wait_ge(s_pe, f1_pe_done + m + 1)
                v.wait_ge(*wo_xf_dma[m])
                if m >= 2:
                    v.wait_ge(*wo_st_mark[m - 2])
                dv.inc(v.scalar_tensor_tensor(accb[par],
                                              psO[par][:],
                                              vecs["bo"][:, m:m + 1],
                                              xfb[par], ALU.add, ALU.add))
            assert dv.n == wo_dv_done


        x1_free_names = [(f"x1_{ix}" if ix < NCD else f"sp{ix - NCD}")
                         for ix in x1_final_spare]

        a16.release("xn", "hsu", "veff", "woe0", "woe1", "vsum", *x1_free_names)
        hbuf = [a16.alloc(f"h_{i}", T, high=True) for i in range(NCI)]
        hbuf = hbuf + [a16.alloc(f"hsp{i}", T, high=True) for i in range(2)]
        uslab = [a16.alloc(f"usl{i}", NCD * 128) for i in range(2)]
        gslab = [a16.alloc(f"gsl{i}", NCD * 128) for i in range(2)]
        sgb = [a16.alloc(f"sgb{i}", T, high=True) for i in range(2)]
        ps_free("psQx", "psO1", "psO0", "psV")
        psU = [ps_alloc(f"psU{i}", [128, 2 * T], F32) for i in range(2)]

        ug_slab_dma = []

        bfly_sched = {i: [] for i in range(NCI)}
        for s5 in range(5):
            step5 = 1 << s5
            for c5 in range(NCI):
                if c5 & step5:
                    continue
                bfly_sched[c5 | (2 * step5 - 1)].append((s5, c5))
        hmul_mark = []
        hmul_only = []
        _m = wo_dv_done
        for i in range(NCI):
            _m += 1
            hmul_only.append(_m)
            _m += 2 * len(bfly_sched[i])
            hmul_mark.append(_m)

        def ug_silu(i):
            return f1_sc_done + i + 1

        upg_base = wo_pe_done

        ug_dv_done = hmul_mark[-1]
        ug_sc_done = f1_sc_done + NCI
        ug_pe_done = wo_pe_done + NCI

        @block.gpsimd
        def _(g):
            g.wait_ge(s_pe, wo_pe_done)
            g.wait_ge(s_dv, wo_dv_done)
            for i in range(NCI):
                if i >= 2:
                    g.wait_ge(s_pe, wo_pe_done + i - 1)
                sw[i % 2].inc(g.dma_start(out=uslab[i % 2],
                                          in_=wup_d[i * 128:(i + 1) * 128, :]), 16)
                sw[i % 2].inc(g.dma_start(out=gslab[i % 2],
                                          in_=wgt_d[i * 128:(i + 1) * 128, :]), 16)
                ug_slab_dma.append((sw[i % 2].h, sw[i % 2].n))

        @block.tensor
        def _(te):
            te.wait_ge(s_dv, wo_dv_done)
            te.wait_ge(s_sc, f1_sc_done)
            for i in range(NCI):
                par = i % 2
                te.wait_ge(*ug_slab_dma[i])
                if i >= 2:
                    te.wait_ge(s_sc, ug_silu(i - 2))
                    te.wait_ge(s_dv, hmul_only[i - 2])
                for hf in range(NH):
                    for k in range(NCD):
                        te.matmul(psU[par][:, hf * HW:(hf + 1) * HW],
                                  uslab[par][:, k * 128:(k + 1) * 128],
                                  x1[x1_slots[k]][:, hf * HW:(hf + 1) * HW],
                                  start=(k == 0), stop=(k == NCD - 1))
                for hf in range(NH):
                    for k in range(NCD):
                        ins = te.matmul(psU[par][:, T + hf * HW: T + (hf + 1) * HW],
                                        gslab[par][:, k * 128:(k + 1) * 128],
                                        x1[x1_slots[k]][:, hf * HW:(hf + 1) * HW],
                                        start=(k == 0), stop=(k == NCD - 1))
                pe.inc(ins)

        @block.scalar
        def _(s):
            for i in range(NCI):
                par = i % 2
                s.wait_ge(s_pe, upg_base + i + 1)
                if i >= 2:
                    s.wait_ge(s_dv, hmul_only[i - 2])
                sc.inc(s.activation(sgb[par], psU[par][:, T:2 * T], AF.Silu,
                                    scale=vecs["sgpe"][:, i:i + 1]))

        h_slots = list(range(NCI))
        h_level = [0] * NCI
        cross_order = []

        @block.vector
        def _(v):
            spare = [NCI, NCI + 1]
            for i in range(NCI):
                par = i % 2
                v.wait_ge(s_pe, upg_base + i + 1)
                v.wait_ge(s_sc, ug_silu(i))
                dv.inc(v.scalar_tensor_tensor(hbuf[i], psU[par][:, 0:T],
                                              vecs["supe"][:, i:i + 1],
                                              sgb[par], ALU.mult, ALU.mult))
                for (s5, c5) in bfly_sched[i]:
                    d5 = c5 | (1 << s5)
                    a, b = h_slots[c5], h_slots[d5]
                    fa, fb = spare
                    v.wait_ge(s_dv, dv.n)
                    dv.inc(v.tensor_add(hbuf[fa], hbuf[a], hbuf[b]))
                    dv.inc(v.tensor_sub(hbuf[fb], hbuf[a], hbuf[b]))
                    spare = [a, b]
                    h_slots[c5], h_slots[d5] = fa, fb
                    h_level[c5] += 1
                    h_level[d5] += 1
                    for cc in (c5, d5):
                        if h_level[cc] == 5:
                            cross_order.append((cc, dv.n))
                assert dv.n == hmul_mark[i]
        assert dv.n == ug_dv_done and sc.n == ug_sc_done and pe.n == ug_pe_done
        assert len(cross_order) == NCI and all(l == 5 for l in h_level)

        for nm6 in ([f"x1_{i2}" for i2 in range(NCD)] + ["sp0", "sp1"]):
            if nm6 not in x1_free_names:
                a16.release(nm6)

        f2_ev = []
        _dvx, _scx = ug_dv_done, ug_sc_done
        for c in range(NCI):
            if c % 2 == 0:
                _dvx += 1
            else:
                _scx += 1
            f2_ev.append((_dvx, _scx))
        f2_dv_done, f2_sc_done = _dvx, _scx
        f2_pe_done = ug_pe_done + NCI

        F2DEPTH = 4 if T >= 512 else 2

        def psq4(j):
            if F2DEPTH == 4:
                return psU[j % 2][:, ((j // 2) % 2) * T:(((j // 2) % 2) + 1) * T]
            return psU[j % 2][:, 0:T]

        cmark = {c: mk for c, mk in cross_order}
        pairs = sorted(range(NCI // 2), key=lambda p: max(cmark[p], cmark[p + 32]))
        out_chunks = []
        for p in pairs:
            out_chunks.append((p, max(cmark[p], cmark[p + 32]), +1))
            out_chunks.append((p + 32, max(cmark[p], cmark[p + 32]), -1))

        @block.tensor
        def _(te):
            te.wait_ge(s_sc, ug_sc_done)
            for j in range(NCI):
                c, mk, sgn = out_chunks[j]
                p = c % 32
                te.wait_ge(s_dv, mk)
                if j >= F2DEPTH:
                    te.wait_ge(s_dv, f2_ev[j - F2DEPTH][0])
                    te.wait_ge(s_sc, f2_ev[j - F2DEPTH][1])
                dst = psq4(j)
                for hf in range(NH):
                    te.matmul(dst[:, hf * HW:(hf + 1) * HW],
                              hdp,
                              hbuf[h_slots[p]][:, hf * HW:(hf + 1) * HW],
                              start=True, stop=False)
                    ins = te.matmul(dst[:, hf * HW:(hf + 1) * HW],
                                    hdp if sgn > 0 else hdpn,
                                    hbuf[h_slots[p + 32]][:, hf * HW:(hf + 1) * HW],
                                    start=False, stop=True)
                pe.inc(ins)

        @block.vector
        def _(v):
            for j in range(0, NCI, 2):
                v.wait_ge(s_pe, ug_pe_done + (j | 1) + 1)
                dv.inc(v.tensor_copy(hbuf[h_slots[out_chunks[j][0]]], psq4(j)))

        @block.scalar
        def _(s):
            for j in range(1, NCI, 2):
                s.wait_ge(s_pe, ug_pe_done + j + 1)
                sc.inc(s.activation(hbuf[h_slots[out_chunks[j][0]]], psq4(j), AF.Identity))
        assert dv.n == f2_dv_done and sc.n == f2_sc_done and pe.n == f2_pe_done
        f2b_dv_done = f2_dv_done

        a16.release("usl0", "usl1", "gsl0", "gsl1", "sgb0", "sgb1")
        a32.release("xfb0", "xfb1", "acb0", "acb1")
        dslab = [a16.alloc(f"dsl{i}", 16 * 128) for i in range(12)]
        xfb2 = [a32.alloc(f"xf2{i}", T) for i in range(2)]
        outb = [a32.alloc(f"out{i}", T) for i in range(2)]

        dn_slab_dma = []
        dn_acc_dma = []
        dn_st_mark = []

        def dn_fin(m):
            return f2b_dv_done + m + 1

        @block.gpsimd
        def _(g):
            g.wait_ge(s_dv, ug_dv_done)
            g.wait_ge(s_pe, ug_pe_done)
            for m in range(NCD):
                if m >= 3:
                    g.wait_ge(s_pe, f2_pe_done + m - 2)
                for kq in range(4):
                    idx = m * 4 + kq
                    sd[m % 3].inc(g.dma_start(
                        out=dslab[(m % 3) * 4 + kq],
                        in_=wdn_d[idx * 128:(idx + 1) * 128, :]), 16)
                dn_slab_dma.append((sd[m % 3].h, sd[m % 3].n))

        @block.sync
        def _(s):
            s.wait_ge(s_dv, f2b_dv_done)
            s.wait_ge(s_sc, f2_sc_done)
            for m in range(NCD):
                if m >= 2:
                    s.wait_ge(s_dv, dn_fin(m - 2))
                sx[m % 2].inc(s.dma_start(out=xfb2[m % 2],
                                          in_=scr_d[m * 128:(m + 1) * 128, :]), 16)
                dn_acc_dma.append((sx[m % 2].h, sx[m % 2].n))
                s.wait_ge(s_dv, dn_fin(m))
                so[m % 2].inc(s.dma_start(out=y_d[m * 128:(m + 1) * 128, :],
                                          in_=outb[m % 2]), 16)
                dn_st_mark.append((so[m % 2].h, so[m % 2].n))
            s.wait_ge(s_o0, so[0].n)
            s.wait_ge(s_o1, so[1].n)

        @block.tensor
        def _(te):
            te.wait_ge(s_dv, f2b_dv_done)
            te.wait_ge(s_sc, f2_sc_done)
            for m in range(NCD):
                par = m % 2
                if m >= 2:
                    te.wait_ge(s_dv, dn_fin(m - 2))
                te.wait_ge(*dn_slab_dma[m])
                for hf in range(NH):
                    for kq in range(4):
                        for k in range(16):
                            kk = kq * 16 + k
                            ins = te.matmul(
                                psU[par][:, hf * HW:(hf + 1) * HW],
                                dslab[(m % 3) * 4 + kq][:, k * 128:(k + 1) * 128],
                                hbuf[h_slots[kk]][:, hf * HW:(hf + 1) * HW],
                                start=(kq == 0 and k == 0), stop=(kq == 3 and k == 15))
                pe.inc(ins)

        @block.vector
        def _(v):
            v.wait_ge(s_o0, wo_so_done[0])
            v.wait_ge(s_o1, wo_so_done[1])
            for m in range(NCD):
                par = m % 2
                v.wait_ge(s_pe, f2_pe_done + m + 1)
                v.wait_ge(*dn_acc_dma[m])
                if m >= 2:
                    v.wait_ge(*dn_st_mark[m - 2])
                dv.inc(v.scalar_tensor_tensor(outb[m % 2],
                                              psU[par][:, 0:T],
                                              vecs["sdpe"][:, m:m + 1],
                                              xfb2[m % 2], ALU.mult, ALU.add))
        ps_free("psU1", "psU0")

    return nc


def _tile_upgate(w):
    wt = np.ascontiguousarray(w.T)
    t = wt.reshape(16, 128, 64, 128).transpose(2, 1, 0, 3)
    return np.ascontiguousarray(t).reshape(64 * 128, 16 * 128)


def _tile_down(w):
    wt = np.ascontiguousarray(w.T)
    t = wt.reshape(4, 16, 128, 16, 128).transpose(3, 0, 2, 1, 4)
    return np.ascontiguousarray(t).reshape(64 * 128, 16 * 128)


def _hadamard(n):
    h = np.array([[1.0]], dtype=np.float64)
    while h.shape[0] < n:
        h = np.block([[h, h], [h, -h]])
    return h


def _vec(v, nch):
    return np.ascontiguousarray(np.asarray(v, np.float32).reshape(nch, 128).T)


def kernel(L0_attn_x, L0_attn_Wq, L0_attn_bq, L0_attn_Wk, L0_attn_bk,
           L0_attn_Wv, L0_attn_bv, L0_attn_Wo, L0_attn_bo,
           L0_ffn_SU_ug, L0_ffn_SU_dp, L0_ffn_Wscale_ug, L0_ffn_Wscale_dp,
           L0_ffn_W_up, L0_ffn_W_gate, L0_ffn_W_down):
    x = np.ascontiguousarray(np.asarray(L0_attn_x, dtype=np.float32)).reshape(-1, D)
    n_tok = x.shape[0]
    T = n_tok // N_CORES

    ws = np.asarray(L0_ffn_Wscale_ug, np.float32)
    shared = dict(
        wv=np.ascontiguousarray(np.asarray(L0_attn_Wv, np.float32).T),
        wo=np.ascontiguousarray(np.asarray(L0_attn_Wo, np.float32).T),
        wup=_tile_upgate(np.asarray(L0_ffn_W_up, np.float32)),
        wgt=_tile_upgate(np.asarray(L0_ffn_W_gate, np.float32)),
        wdn=_tile_down(np.asarray(L0_ffn_W_down, np.float32)),
        bv=_vec(L0_attn_bv, NCD), bo=_vec(L0_attn_bo, NCD),
        suug=_vec(L0_ffn_SU_ug, NCD), sudp=_vec(L0_ffn_SU_dp, NCI),
        wsu=_vec(ws[:I], NCI), wsg=_vec(ws[I:], NCI),
        wsd=_vec(L0_ffn_Wscale_dp, NCD),
        h1c=(_hadamard(128) * FWHT_A).astype(np.float32),
        h2c=(_hadamard(128) * FWHT_A).astype(np.float32),
        identc=np.eye(128, dtype=np.float16),
    )

    nc = build_program(T)
    in_maps = []
    for ci in range(N_CORES):
        xs = x[ci * T:(ci + 1) * T]
        m = dict(shared)
        m["x_tm"] = np.ascontiguousarray(xs)
        m["x_fm"] = np.ascontiguousarray(xs.T)
        in_maps.append(m)

    res = run_bass_kernel_spmd(nc, in_maps, core_ids=list(range(N_CORES)))
    y = np.concatenate([res.results[ci]["y"] for ci in range(N_CORES)], axis=1)
    return np.ascontiguousarray(y.T).reshape(np.asarray(L0_attn_x).shape).astype(np.float32)


# version 22
# speedup vs baseline: 1.0347x; 1.0003x over previous
import numpy as np

import concourse.bass as bass
import concourse.mybir as mybir
from concourse.bass_utils import run_bass_kernel_spmd

F32 = mybir.dt.float32
F16 = mybir.dt.float16
AF = mybir.ActivationFunctionType
ALU = mybir.AluOpType
AX = mybir.AxisListType.X

D = 2048
H = 16
I = 8192
NCD = D // 128
NCI = I // 128
N_CORES = 8
EPS = 1e-5
FWHT_A = 0.0004873637808486819
FWHT_K1 = 45.34002685546875
FWHT_K2 = 22.670013427734375


class Ctr:
    def __init__(self, handle):
        self.h = handle
        self.n = 0

    def inc(self, ins, k=1):
        ins.then_inc(self.h, k)
        self.n += k
        return self.n


class Arena:

    def __init__(self, ap, ncols):
        self.ap = ap
        self.avail = [(0, ncols)]
        self.used = {}

    def alloc(self, name, n, high=False):
        idxs = range(len(self.avail) - 1, -1, -1) if high else range(len(self.avail))
        for i in idxs:
            s, l = self.avail[i]
            if l >= n:
                if high:
                    start = s + l - n
                    self.used[name] = (start, n)
                    if l == n:
                        self.avail.pop(i)
                    else:
                        self.avail[i] = (s, l - n)
                    return self.ap[:, start:start + n]
                self.used[name] = (s, n)
                if l == n:
                    self.avail.pop(i)
                else:
                    self.avail[i] = (s + n, l - n)
                return self.ap[:, s:s + n]
        raise MemoryError(f"arena: {name} needs {n} cols; free={self.avail}")

    def release(self, *names):
        for name in names:
            s, n = self.used.pop(name)
            self.avail.append((s, n))
        self.avail.sort()
        merged = []
        for s, l in self.avail:
            if merged and merged[-1][0] + merged[-1][1] == s:
                merged[-1] = (merged[-1][0], merged[-1][1] + l)
            else:
                merged.append((s, l))
        self.avail = merged


def build_program(T):
    NT = T // 128
    HW = min(512, T)
    NH = T // HW
    nc = bass.Bass("TRN2", target_bir_lowering=False, debug=False)

    x_tm_d = nc.dram_tensor("x_tm", [T, D], F32, kind="ExternalInput").ap()
    x_fm_d = nc.dram_tensor("x_fm", [D, T], F32, kind="ExternalInput").ap()
    wv_d = nc.dram_tensor("wv", [D, D], F32, kind="ExternalInput").ap()
    wo_d = nc.dram_tensor("wo", [D, D], F32, kind="ExternalInput").ap()
    wup_d = nc.dram_tensor("wup", [NCI * 128, NCD * 128], F32, kind="ExternalInput").ap()
    wgt_d = nc.dram_tensor("wgt", [NCI * 128, NCD * 128], F32, kind="ExternalInput").ap()
    wdn_d = nc.dram_tensor("wdn", [64 * 128, 16 * 128], F32, kind="ExternalInput").ap()
    vec_d = {}
    for nm, w in (("bv", NCD), ("bo", NCD), ("suug", NCD), ("sudp", NCI),
                  ("wsu", NCI), ("wsg", NCI), ("wsd", NCD)):
        vec_d[nm] = nc.dram_tensor(nm, [128, w], F32, kind="ExternalInput").ap()
    h1_d = nc.dram_tensor("h1c", [128, 128], F32, kind="ExternalInput").ap()
    h2_d = nc.dram_tensor("h2c", [128, 128], F32, kind="ExternalInput").ap()
    id_d = nc.dram_tensor("identc", [128, 128], F16, kind="ExternalInput").ap()
    y_d = nc.dram_tensor("y", [D, T], F32, kind="ExternalOutput").ap()
    scr_d = nc.dram_tensor("scr", [D, T], F32, kind="Internal").ap()

    A16 = 93 * 1024 + 512
    A32 = 4416

    live_ps = []

    def ps_alloc(name, shape, dt):
        cm = nc.psum_tensor(name, shape, dt)
        t = cm.__enter__()
        live_ps.append((name, cm))
        return t

    def ps_free(*names):
        for name in names:
            nm, cm = live_ps.pop()
            assert nm == name, f"psum free order: {nm} != {name}"
            cm.__exit__(None, None, None)

    from contextlib import ExitStack
    _es = ExitStack()

    def _sem(nm):
        return _es.enter_context(nc.semaphore(nm))

    with _es:
        arena16_t = _es.enter_context(nc.sbuf_tensor("arena16", [128, A16], F16))
        arena32_t = _es.enter_context(nc.sbuf_tensor("arena32", [128, A32], F32))
        hdp_t = _es.enter_context(nc.sbuf_tensor("hdp", [128, 128], F16))
        hdpn_t = _es.enter_context(nc.sbuf_tensor("hdpn", [128, 128], F16))
        ident_t = _es.enter_context(nc.sbuf_tensor("ident", [128, 128], F16))
        vecs_t = _es.enter_context(nc.sbuf_tensor("vecs", [128, NCD * 5 + NCI * 5 + 2], F32))
        stat_t = _es.enter_context(nc.sbuf_tensor("stat", [128, 64], F32))
        s_gp = _sem("s_gp")
        s_w0 = _sem("s_w0")
        s_w1 = _sem("s_w1")
        s_x0 = _sem("s_x0")
        s_x1 = _sem("s_x1")
        s_o0 = _sem("s_o0")
        s_o1 = _sem("s_o1")
        s_pe = _sem("s_pe")
        s_dv = _sem("s_dv")
        s_sc = _sem("s_sc")
        s_gs = _sem("s_gs")
        s_d0 = _sem("s_d0")
        s_d1 = _sem("s_d1")
        s_d2 = _sem("s_d2")
        block = _es.enter_context(nc.Block())
        a16 = Arena(arena16_t[:], A16)
        a32 = Arena(arena32_t[:], A32)
        hdp = hdp_t[:]
        hdpn = hdpn_t[:]
        ident = ident_t[:]
        stat = stat_t[:]
        vecs = {}
        off = 0
        for nm, w in (("bv", NCD), ("bo", NCD), ("suug", NCD), ("wsd", NCD),
                      ("sdpe", NCD), ("sudp", NCI), ("wsu", NCI), ("wsg", NCI),
                      ("supe", NCI), ("sgpe", NCI), ("bveff", 1), ("pad", 1)):
            vecs[nm] = vecs_t[:, off:off + w]
            off += w
        gp = Ctr(s_gp)
        sw = [Ctr(s_w0), Ctr(s_w1)]
        sx = [Ctr(s_x0), Ctr(s_x1)]
        so = [Ctr(s_o0), Ctr(s_o1)]
        pe = Ctr(s_pe)
        dv = Ctr(s_dv)
        sc = Ctr(s_sc)
        gs = Ctr(s_gs)
        sd = [Ctr(s_d0), Ctr(s_d1), Ctr(s_d2)]

        hsu = a16.alloc("hsu", NCD * 128)
        h1sb = a32.alloc("h1sb", 128)
        h2sb = a32.alloc("h2sb", 128)
        tmp64 = a32.alloc("tmp64", NCI)

        @block.gpsimd
        def _(g):
            for dst, src in ((ident, id_d), (h1sb, h1_d), (h2sb, h2_d)):
                gp.inc(g.dma_start(out=dst, in_=src[:]), 16)
            for nm in ("bv", "bo", "suug", "sudp", "wsu", "wsg", "wsd"):
                gp.inc(g.dma_start(out=vecs[nm], in_=vec_d[nm][:]), 16)
        const_dma = gp.n

        @block.vector
        def _(v):
            v.wait_ge(s_gp, const_dma)
            for c in range(NCD):
                dv.inc(v.tensor_scalar(hsu[:, c * 128:(c + 1) * 128], h1sb,
                                       vecs["suug"][:, c:c + 1], None, ALU.mult))
            dv.inc(v.tensor_copy(hdp, h2sb))
            dv.inc(v.tensor_scalar(hdpn, h2sb, -1.0, None, ALU.mult))
            dv.inc(v.tensor_scalar(tmp64, vecs["wsu"], FWHT_K1, None, ALU.mult))
            dv.inc(v.reduce_sum(vecs["bveff"], vecs["bv"], axis=AX))
            v.wait_ge(s_dv, dv.n)
            dv.inc(v.tensor_mul(vecs["supe"], tmp64, vecs["sudp"]))
            dv.inc(v.tensor_scalar(vecs["sgpe"], vecs["wsg"], FWHT_K1, None, ALU.mult))
            dv.inc(v.tensor_scalar(vecs["sdpe"], vecs["wsd"], FWHT_K2, None, ALU.mult))
        const_dv = dv.n

        xn = a16.alloc("xn", NCD * T)
        xn_tm_b = [a16.alloc(f"xn_tm{i}", D) for i in range(2)]
        xt_b = [a32.alloc(f"xt{i}", D) for i in range(2)]
        veff = a16.alloc("veff", NCD * 128)
        woeff = [a16.alloc(f"woe{i}", D) for i in range(2)]
        vwsl = [a16.alloc(f"vwsl{i}", 2 * D) for i in range(2)]
        vtmp = a16.alloc("vtmp", 3584)
        pstr = [ps_alloc(f"pstr{i}", [128, 1024], F16) for i in range(2)]

        NPAIR = NCD // 2
        NFP = max(1, NPAIR // NT)
        assert NFP * NT == NPAIR or NT == 1
        ln_sc_done = 2 * NT
        ln_pe_done = 2 * NT

        sqrt_sc = [0] * NT
        evac1_sc = [0] * NT
        _scpos = 0
        for _t in range(NT):
            _scpos += 1
            sqrt_sc[_t] = _scpos
            if _t >= 1:
                _scpos += 1
                evac1_sc[_t - 1] = _scpos
        _scpos += 1
        evac1_sc[NT - 1] = _scpos

        eps_mark = [0] * NT
        apply_mark = [0] * NT
        woadd_mark = [0] * NPAIR
        _dvpos = const_dv
        for _t in range(NT + 1):
            if _t < NT:
                _dvpos += 6
                eps_mark[_t] = _dvpos
            if _t >= 1:
                _tt = _t - 1
                _dvpos += 3
                apply_mark[_tt] = _dvpos
                _dvpos += 1
                for _jf in range(NFP):
                    _p = _tt * NFP + _jf
                    if _p < NPAIR:
                        _dvpos += 2
                        woadd_mark[_p] = _dvpos
        ln_dv_done = _dvpos

        wo_final = woeff[(NPAIR * 2 - 1) % 2]
        vw_mark = []

        @block.gpsimd
        def _(g):
            for p in range(NPAIR):
                if p >= 1:
                    g.wait_ge(s_gs, 4 * p)
                    g.wait_ge(s_dv, woadd_mark[p - 1])
                for half in range(2):
                    k = 2 * p + half
                    sw[0].inc(g.dma_start(out=vwsl[0][:, half * D:(half + 1) * D],
                                          in_=wv_d[k * 128:(k + 1) * 128, :]), 16)
                    sw[1].inc(g.dma_start(out=vwsl[1][:, half * D:(half + 1) * D],
                                          in_=wo_d[k * 128:(k + 1) * 128, :]), 16)
                vw_mark.append((sw[0].n, sw[1].n))
                g.wait_ge(s_w0, sw[0].n)
                v0 = vwsl[0].rearrange("p (b a two d) -> p b a two d",
                                       b=2, two=2, d=128)
                vt1 = vtmp[:, 0:2048].rearrange("p (b a d) -> p b a d", b=2, d=128)
                gs.inc(g.tensor_add(vt1, v0[:, :, :, 0, :], v0[:, :, :, 1, :]))
                g.wait_ge(s_gs, gs.n)
                t1f = vtmp[:, 0:2048].rearrange("p (b a two d) -> p b a two d",
                                                b=2, two=2, d=128)
                vt2 = vtmp[:, 2048:3072].rearrange("p (b a d) -> p b a d", b=2, d=128)
                gs.inc(g.tensor_add(vt2, t1f[:, :, :, 0, :], t1f[:, :, :, 1, :]))
                g.wait_ge(s_gs, gs.n)
                t2f = vtmp[:, 2048:3072].rearrange("p (b a two d) -> p b a two d",
                                                   b=2, two=2, d=128)
                vt3 = vtmp[:, 3072:3584].rearrange("p (b a d) -> p b a d", b=2, d=128)
                gs.inc(g.tensor_add(vt3, t2f[:, :, :, 0, :], t2f[:, :, :, 1, :]))
                g.wait_ge(s_gs, gs.n)
                t3f = vtmp[:, 3072:3584].rearrange("p (b two d) -> p b two d",
                                                   b=2, d=128)
                gs.inc(g.tensor_add(
                    veff[:, 2 * p * 128:(2 * p + 2) * 128].rearrange(
                        "p (b one d) -> p b one d", one=1, d=128),
                    t3f[:, :, 0:1, :], t3f[:, :, 1:2, :]))
        veff_done = gs.n
        assert veff_done == 4 * NPAIR

        xt_mark = []

        @block.sync
        def _(s):
            for t in range(NT):
                if t >= 2:
                    s.wait_ge(s_dv, apply_mark[t - 2])
                sx[t % 2].inc(s.dma_start(out=xt_b[t % 2],
                                          in_=x_tm_d[t * 128:(t + 1) * 128, :]), 16)
                xt_mark.append((sx[t % 2].h, sx[t % 2].n))

        @block.scalar
        def _(s):
            def evac1(t):
                s.wait_ge(s_pe, 2 * (t + 1))
                dst = xn.rearrange("p (c t2) -> p c t2", c=NCD)[
                    :, 8:16, t * 128:(t + 1) * 128]
                sc.inc(s.activation(dst, pstr[1][:].rearrange(
                    "p (c t2) -> p c t2", c=8), AF.Identity))
            for t in range(NT):
                s.wait_ge(s_dv, eps_mark[t])
                sc.inc(s.activation(stat[:, 5 * t + 4:5 * t + 5],
                                    stat[:, 5 * t + 1:5 * t + 2], AF.Sqrt))
                if t >= 1:
                    evac1(t - 1)
            evac1(NT - 1)
        assert sc.n == ln_sc_done

        @block.vector
        def _(v):
            NB = D // 512
            for t in range(NT + 1):
                if t < NT:
                    v.wait_ge(*xt_mark[t])
                    for q in range(NB):
                        dv.inc(v.bn_stats(stat[:, 40 + q * 6:40 + (q + 1) * 6],
                                          xt_b[t % 2][:, q * 512:(q + 1) * 512]))
                    v.wait_ge(s_dv, dv.n)
                    dv.inc(v.bn_aggr(stat[:, 5 * t:5 * t + 2], stat[:, 40:40 + 6 * NB]))
                    v.wait_ge(s_dv, dv.n)
                    dv.inc(v.tensor_scalar(stat[:, 5 * t + 1:5 * t + 2],
                                           stat[:, 5 * t + 1:5 * t + 2], EPS, None, ALU.add))
                    assert dv.n == eps_mark[t]
                if t >= 1:
                    tt = t - 1
                    v.wait_ge(s_sc, sqrt_sc[tt])
                    dv.inc(v.reciprocal(stat[:, 5 * tt + 2:5 * tt + 3],
                                        stat[:, 5 * tt + 4:5 * tt + 5]))
                    v.wait_ge(s_dv, dv.n)
                    dv.inc(v.scalar_tensor_tensor(stat[:, 5 * tt + 3:5 * tt + 4],
                                                  stat[:, 5 * tt:5 * tt + 1], -1.0,
                                                  stat[:, 5 * tt + 2:5 * tt + 3],
                                                  ALU.mult, ALU.mult))
                    v.wait_ge(s_dv, dv.n)
                    dv.inc(v.tensor_scalar(xn_tm_b[tt % 2], xt_b[tt % 2],
                                           stat[:, 5 * tt + 2:5 * tt + 3],
                                           stat[:, 5 * tt + 3:5 * tt + 4],
                                           ALU.mult, ALU.add))
                    assert dv.n == apply_mark[tt]
                    v.wait_ge(s_pe, 2 * (tt + 1))
                    dst = xn.rearrange("p (c t2) -> p c t2", c=NCD)[
                        :, 0:8, tt * 128:(tt + 1) * 128]
                    dv.inc(v.tensor_copy(dst, pstr[0][:].rearrange(
                        "p (c t2) -> p c t2", c=8)))
                    for jf in range(NFP):
                        p = tt * NFP + jf
                        if p >= NPAIR:
                            continue
                        v.wait_ge(s_w1, vw_mark[p][1])
                        for half in range(2):
                            k = 2 * p + half
                            if k == 0:
                                dv.inc(v.tensor_copy(woeff[0], vwsl[1][:, 0:D]))
                            else:
                                v.wait_ge(s_dv, dv.n)
                                dv.inc(v.tensor_add(woeff[k % 2], woeff[(k - 1) % 2],
                                                    vwsl[1][:, half * D:(half + 1) * D]))
                        assert dv.n == woadd_mark[p]
        assert dv.n == ln_dv_done

        @block.tensor
        def _(te):
            for t in range(NT):
                te.wait_ge(s_dv, apply_mark[t])
                if t >= 1:
                    te.wait_ge(s_sc, evac1_sc[t - 1])
                for gb in range(2):
                    for c in range(8):
                        cc = gb * 8 + c
                        ins = te.transpose(pstr[gb][:, c * 128:(c + 1) * 128],
                                           xn_tm_b[t % 2][:, cc * 128:(cc + 1) * 128],
                                           ident)
                    pe.inc(ins)
        assert pe.n == ln_pe_done and dv.n == ln_dv_done and sc.n == ln_sc_done

        vsum = a16.alloc("vsum", T)
        a16.release("xn_tm0", "xn_tm1")
        a32.release("xt0", "xt1", "h1sb", "h2sb", "tmp64")
        ps_free("pstr1", "pstr0")
        psV = ps_alloc("psV", [128, T], F32)
        psO = [ps_alloc(f"psO{i}", [128, T], F32) for i in range(2)]
        psQx = ps_alloc("psQx", [128, T], F32)
        xfb = [a32.alloc(f"xfb{i}", T) for i in range(2)]
        accb = [a32.alloc(f"acb{i}", T) for i in range(2)]
        a16.release("vwsl0", "vwsl1", "vtmp")
        x1 = [a16.alloc(f"x1_{i}", T, high=True) for i in range(NCD)]
        x1 = x1 + [a16.alloc(f"sp{i}", T, high=True) for i in range(2)]

        vs_pe_done = ln_pe_done + 1
        vs_dv_done = ln_dv_done + 1
        f1_pe_done = vs_pe_done + NCD
        wo_pe_done = f1_pe_done + NCD
        bf1_sched = {c: [] for c in range(NCD)}
        for _s in range(4):
            _st = 1 << _s
            for _c in range(NCD):
                if _c & _st:
                    continue
                bf1_sched[_c | (2 * _st - 1)].append((_s, _c))
        f1_ev = []
        _dvx, _scx = vs_dv_done, ln_sc_done
        for c in range(NCD):
            if c % 2 == 0:
                _dvx += 1
                f1_ev.append((s_dv, _dvx))
            else:
                _scx += 1
                f1_ev.append((s_sc, _scx))
            _dvx += 2 * len(bf1_sched[c])
        f1_dv_done, f1_sc_done = _dvx, _scx
        f1b_dv_done = f1_dv_done

        def wo_acc_mark(m):
            return f1b_dv_done + m + 1

        wo_dv_done = f1b_dv_done + NCD

        @block.tensor
        def _(te):
            te.wait_ge(s_dv, ln_dv_done)
            te.wait_ge(s_sc, ln_sc_done)
            te.wait_ge(s_gs, veff_done)
            for hf in range(NH):
                for k in range(NCD):
                    ins = te.matmul(psV[:, hf * HW:(hf + 1) * HW],
                                    veff[:, k * 128:(k + 1) * 128],
                                    xn[:, k * T + hf * HW: k * T + (hf + 1) * HW],
                                    start=(k == 0), stop=(k == NCD - 1))
            pe.inc(ins)
            x1ps = [psQx, psV, psO[0], psO[1]]
            for c in range(NCD):
                if c == 1:
                    te.wait_ge(s_dv, vs_dv_done)
                if c >= 4:
                    te.wait_ge(f1_ev[c - 4][0], f1_ev[c - 4][1])
                dstp = x1ps[c % 4]
                for hf in range(NH):
                    ins = te.matmul(dstp[:, hf * HW:(hf + 1) * HW],
                                    hsu[:, c * 128:(c + 1) * 128],
                                    xn[:, c * T + hf * HW: c * T + (hf + 1) * HW],
                                    start=True, stop=True)
                pe.inc(ins)
            te.wait_ge(s_dv, vs_dv_done)
            for m in range(NCD):
                par = m % 2
                if m == 0:
                    te.wait_ge(*f1_ev[NCD - 2])
                    te.wait_ge(*f1_ev[NCD - 1])
                if m >= 2:
                    te.wait_ge(s_dv, wo_acc_mark(m - 2))
                for hf in range(NH):
                    ins = te.matmul(psO[par][:, hf * HW:(hf + 1) * HW],
                                    wo_final[:, m * 128:(m + 1) * 128],
                                    vsum[:, hf * HW:(hf + 1) * HW],
                                    start=True, stop=True)
                pe.inc(ins)
        assert pe.n == wo_pe_done

        @block.scalar
        def _(s):
            for c in range(1, NCD, 2):
                s.wait_ge(s_pe, vs_pe_done + c + 1)
                sc.inc(s.activation(x1[c], (psV if c % 4 == 1 else psO[1])[:],
                                    AF.Identity))
        assert sc.n == f1_sc_done

        x1_slots = list(range(NCD))
        x1_final_spare = []

        wo_xf_dma = []
        wo_st_mark = []

        @block.sync
        def _(s):
            s.wait_ge(s_dv, vs_dv_done)
            s.wait_ge(s_sc, ln_sc_done)
            for m in range(NCD):
                if m >= 2:
                    s.wait_ge(s_dv, wo_acc_mark(m - 2))
                sx[m % 2].inc(s.dma_start(out=xfb[m % 2],
                                          in_=x_fm_d[m * 128:(m + 1) * 128, :]), 16)
                wo_xf_dma.append((sx[m % 2].h, sx[m % 2].n))
                s.wait_ge(s_dv, wo_acc_mark(m))
                so[m % 2].inc(s.dma_start(out=scr_d[m * 128:(m + 1) * 128, :],
                                          in_=accb[m % 2]), 16)
                wo_st_mark.append((so[m % 2].h, so[m % 2].n))
        wo_so_done = (so[0].n, so[1].n)

        @block.vector
        def _(v):
            v.wait_ge(s_pe, vs_pe_done)
            dv.inc(v.tensor_scalar(vsum, psV[:], vecs["bveff"], None, ALU.add))
            spare = [NCD, NCD + 1]
            for c in range(NCD):
                if c % 2 == 0:
                    v.wait_ge(s_pe, vs_pe_done + c + 1)
                    dv.inc(v.tensor_copy(x1[c], (psQx if c % 4 == 0
                                                 else psO[0])[:]))
                else:
                    v.wait_ge(s_sc, f1_ev[c][1])
                for (s5, c5) in bf1_sched[c]:
                    d5 = c5 | (1 << s5)
                    a, b = x1_slots[c5], x1_slots[d5]
                    fa, fb = spare
                    v.wait_ge(s_dv, dv.n)
                    dv.inc(v.tensor_add(x1[fa], x1[a], x1[b]))
                    dv.inc(v.tensor_sub(x1[fb], x1[a], x1[b]))
                    spare = [a, b]
                    x1_slots[c5], x1_slots[d5] = fa, fb
            x1_final_spare.extend(spare)
            assert dv.n == f1b_dv_done
            for m in range(NCD):
                par = m % 2
                v.wait_ge(s_pe, f1_pe_done + m + 1)
                v.wait_ge(*wo_xf_dma[m])
                if m >= 2:
                    v.wait_ge(*wo_st_mark[m - 2])
                dv.inc(v.scalar_tensor_tensor(accb[par],
                                              psO[par][:],
                                              vecs["bo"][:, m:m + 1],
                                              xfb[par], ALU.add, ALU.add))
            assert dv.n == wo_dv_done


        x1_free_names = [(f"x1_{ix}" if ix < NCD else f"sp{ix - NCD}")
                         for ix in x1_final_spare]

        a16.release("xn", "hsu", "veff", "woe0", "woe1", "vsum", *x1_free_names)
        hbuf = [a16.alloc(f"h_{i}", T, high=True) for i in range(NCI)]
        hbuf = hbuf + [a16.alloc(f"hsp{i}", T, high=True) for i in range(2)]
        uslab = [a16.alloc(f"usl{i}", NCD * 128) for i in range(2)]
        gslab = [a16.alloc(f"gsl{i}", NCD * 128) for i in range(2)]
        sgb = [a16.alloc(f"sgb{i}", T, high=True) for i in range(2)]
        ps_free("psQx", "psO1", "psO0", "psV")
        psU = [ps_alloc(f"psU{i}", [128, 2 * T], F32) for i in range(2)]

        ug_slab_dma = []

        bfly_sched = {i: [] for i in range(NCI)}
        for s5 in range(5):
            step5 = 1 << s5
            for c5 in range(NCI):
                if c5 & step5:
                    continue
                bfly_sched[c5 | (2 * step5 - 1)].append((s5, c5))
        hmul_mark = []
        hmul_only = []
        _m = wo_dv_done
        for i in range(NCI):
            _m += 1
            hmul_only.append(_m)
            _m += 2 * len(bfly_sched[i])
            hmul_mark.append(_m)

        def ug_silu(i):
            return f1_sc_done + i + 1

        upg_base = wo_pe_done

        ug_dv_done = hmul_mark[-1]
        ug_sc_done = f1_sc_done + NCI
        ug_pe_done = wo_pe_done + NCI

        @block.gpsimd
        def _(g):
            g.wait_ge(s_pe, wo_pe_done)
            g.wait_ge(s_dv, wo_dv_done)
            for i in range(NCI):
                if i >= 2:
                    g.wait_ge(s_pe, wo_pe_done + i - 1)
                sw[i % 2].inc(g.dma_start(out=uslab[i % 2],
                                          in_=wup_d[i * 128:(i + 1) * 128, :]), 16)
                sw[i % 2].inc(g.dma_start(out=gslab[i % 2],
                                          in_=wgt_d[i * 128:(i + 1) * 128, :]), 16)
                ug_slab_dma.append((sw[i % 2].h, sw[i % 2].n))

        @block.tensor
        def _(te):
            te.wait_ge(s_dv, wo_dv_done)
            te.wait_ge(s_sc, f1_sc_done)
            for i in range(NCI):
                par = i % 2
                te.wait_ge(*ug_slab_dma[i])
                if i >= 2:
                    te.wait_ge(s_sc, ug_silu(i - 2))
                    te.wait_ge(s_dv, hmul_only[i - 2])
                for hf in range(NH):
                    for k in range(NCD):
                        te.matmul(psU[par][:, hf * HW:(hf + 1) * HW],
                                  uslab[par][:, k * 128:(k + 1) * 128],
                                  x1[x1_slots[k]][:, hf * HW:(hf + 1) * HW],
                                  start=(k == 0), stop=(k == NCD - 1))
                for hf in range(NH):
                    for k in range(NCD):
                        ins = te.matmul(psU[par][:, T + hf * HW: T + (hf + 1) * HW],
                                        gslab[par][:, k * 128:(k + 1) * 128],
                                        x1[x1_slots[k]][:, hf * HW:(hf + 1) * HW],
                                        start=(k == 0), stop=(k == NCD - 1))
                pe.inc(ins)

        @block.scalar
        def _(s):
            for i in range(NCI):
                par = i % 2
                s.wait_ge(s_pe, upg_base + i + 1)
                if i >= 2:
                    s.wait_ge(s_dv, hmul_only[i - 2])
                sc.inc(s.activation(sgb[par], psU[par][:, T:2 * T], AF.Silu,
                                    scale=vecs["sgpe"][:, i:i + 1]))

        h_slots = list(range(NCI))
        h_level = [0] * NCI
        cross_order = []

        @block.vector
        def _(v):
            spare = [NCI, NCI + 1]
            for i in range(NCI):
                par = i % 2
                v.wait_ge(s_pe, upg_base + i + 1)
                v.wait_ge(s_sc, ug_silu(i))
                dv.inc(v.scalar_tensor_tensor(hbuf[i], psU[par][:, 0:T],
                                              vecs["supe"][:, i:i + 1],
                                              sgb[par], ALU.mult, ALU.mult))
                for (s5, c5) in bfly_sched[i]:
                    d5 = c5 | (1 << s5)
                    a, b = h_slots[c5], h_slots[d5]
                    fa, fb = spare
                    v.wait_ge(s_dv, dv.n)
                    dv.inc(v.tensor_add(hbuf[fa], hbuf[a], hbuf[b]))
                    dv.inc(v.tensor_sub(hbuf[fb], hbuf[a], hbuf[b]))
                    spare = [a, b]
                    h_slots[c5], h_slots[d5] = fa, fb
                    h_level[c5] += 1
                    h_level[d5] += 1
                    for cc in (c5, d5):
                        if h_level[cc] == 5:
                            cross_order.append((cc, dv.n))
                assert dv.n == hmul_mark[i]
        assert dv.n == ug_dv_done and sc.n == ug_sc_done and pe.n == ug_pe_done
        assert len(cross_order) == NCI and all(l == 5 for l in h_level)

        for nm6 in ([f"x1_{i2}" for i2 in range(NCD)] + ["sp0", "sp1"]):
            if nm6 not in x1_free_names:
                a16.release(nm6)

        f2_ev = []
        _dvx, _scx = ug_dv_done, ug_sc_done
        for c in range(NCI):
            if c % 2 == 0:
                _dvx += 1
            else:
                _scx += 1
            f2_ev.append((_dvx, _scx))
        f2_dv_done, f2_sc_done = _dvx, _scx
        f2_pe_done = ug_pe_done + NCI

        F2DEPTH = 4 if T >= 512 else 2

        def psq4(j):
            if F2DEPTH == 4:
                return psU[j % 2][:, ((j // 2) % 2) * T:(((j // 2) % 2) + 1) * T]
            return psU[j % 2][:, 0:T]

        cmark = {c: mk for c, mk in cross_order}
        pairs = sorted(range(NCI // 2), key=lambda p: max(cmark[p], cmark[p + 32]))
        out_chunks = []
        for p in pairs:
            out_chunks.append((p, max(cmark[p], cmark[p + 32]), +1))
            out_chunks.append((p + 32, max(cmark[p], cmark[p + 32]), -1))

        @block.tensor
        def _(te):
            te.wait_ge(s_sc, ug_sc_done)
            for j in range(NCI):
                c, mk, sgn = out_chunks[j]
                p = c % 32
                te.wait_ge(s_dv, mk)
                if j >= F2DEPTH:
                    te.wait_ge(s_dv, f2_ev[j - F2DEPTH][0])
                    te.wait_ge(s_sc, f2_ev[j - F2DEPTH][1])
                dst = psq4(j)
                for hf in range(NH):
                    te.matmul(dst[:, hf * HW:(hf + 1) * HW],
                              hdp,
                              hbuf[h_slots[p]][:, hf * HW:(hf + 1) * HW],
                              start=True, stop=False)
                    ins = te.matmul(dst[:, hf * HW:(hf + 1) * HW],
                                    hdp if sgn > 0 else hdpn,
                                    hbuf[h_slots[p + 32]][:, hf * HW:(hf + 1) * HW],
                                    start=False, stop=True)
                pe.inc(ins)

        @block.vector
        def _(v):
            for j in range(0, NCI, 2):
                v.wait_ge(s_pe, ug_pe_done + (j | 1) + 1)
                dv.inc(v.tensor_copy(hbuf[h_slots[out_chunks[j][0]]], psq4(j)))

        @block.scalar
        def _(s):
            for j in range(1, NCI, 2):
                s.wait_ge(s_pe, ug_pe_done + j + 1)
                sc.inc(s.activation(hbuf[h_slots[out_chunks[j][0]]], psq4(j), AF.Identity))
        assert dv.n == f2_dv_done and sc.n == f2_sc_done and pe.n == f2_pe_done
        f2b_dv_done = f2_dv_done

        a16.release("usl0", "usl1", "gsl0", "gsl1", "sgb0", "sgb1")
        a32.release("xfb0", "xfb1", "acb0", "acb1")
        dslab = [a16.alloc(f"dsl{i}", 16 * 128) for i in range(12)]
        xfb2 = [a32.alloc(f"xf2{i}", T) for i in range(2)]
        outb = [a32.alloc(f"out{i}", T) for i in range(2)]

        dn_slab_dma = []
        dn_acc_dma = []
        dn_st_mark = []

        def dn_fin(m):
            return f2b_dv_done + m + 1

        @block.gpsimd
        def _(g):
            g.wait_ge(s_dv, ug_dv_done)
            g.wait_ge(s_pe, ug_pe_done)
            for m in range(NCD):
                if m >= 3:
                    g.wait_ge(s_pe, f2_pe_done + m - 2)
                for kq in range(4):
                    idx = m * 4 + kq
                    sd[m % 3].inc(g.dma_start(
                        out=dslab[(m % 3) * 4 + kq],
                        in_=wdn_d[idx * 128:(idx + 1) * 128, :]), 16)
                dn_slab_dma.append((sd[m % 3].h, sd[m % 3].n))

        @block.sync
        def _(s):
            s.wait_ge(s_dv, f2b_dv_done)
            s.wait_ge(s_sc, f2_sc_done)
            for m in range(NCD):
                if m >= 2:
                    s.wait_ge(s_dv, dn_fin(m - 2))
                sx[m % 2].inc(s.dma_start(out=xfb2[m % 2],
                                          in_=scr_d[m * 128:(m + 1) * 128, :]), 16)
                dn_acc_dma.append((sx[m % 2].h, sx[m % 2].n))
                s.wait_ge(s_dv, dn_fin(m))
                so[m % 2].inc(s.dma_start(out=y_d[m * 128:(m + 1) * 128, :],
                                          in_=outb[m % 2]), 16)
                dn_st_mark.append((so[m % 2].h, so[m % 2].n))
            s.wait_ge(s_o0, so[0].n)
            s.wait_ge(s_o1, so[1].n)

        @block.tensor
        def _(te):
            te.wait_ge(s_dv, f2b_dv_done)
            te.wait_ge(s_sc, f2_sc_done)
            for m in range(NCD):
                par = m % 2
                if m >= 2:
                    te.wait_ge(s_dv, dn_fin(m - 2))
                te.wait_ge(*dn_slab_dma[m])
                for hf in range(NH):
                    for kq in range(4):
                        for k in range(16):
                            kk = kq * 16 + k
                            ins = te.matmul(
                                psU[par][:, hf * HW:(hf + 1) * HW],
                                dslab[(m % 3) * 4 + kq][:, k * 128:(k + 1) * 128],
                                hbuf[h_slots[kk]][:, hf * HW:(hf + 1) * HW],
                                start=(kq == 0 and k == 0), stop=(kq == 3 and k == 15))
                pe.inc(ins)

        @block.vector
        def _(v):
            v.wait_ge(s_o0, wo_so_done[0])
            v.wait_ge(s_o1, wo_so_done[1])
            for m in range(NCD):
                par = m % 2
                v.wait_ge(s_pe, f2_pe_done + m + 1)
                v.wait_ge(*dn_acc_dma[m])
                if m >= 2:
                    v.wait_ge(*dn_st_mark[m - 2])
                dv.inc(v.scalar_tensor_tensor(outb[m % 2],
                                              psU[par][:, 0:T],
                                              vecs["sdpe"][:, m:m + 1],
                                              xfb2[m % 2], ALU.mult, ALU.add))
        ps_free("psU1", "psU0")

    return nc


def _tile_upgate(w):
    wt = np.ascontiguousarray(w.T)
    t = wt.reshape(16, 128, 64, 128).transpose(2, 1, 0, 3)
    return np.ascontiguousarray(t).reshape(64 * 128, 16 * 128)


def _tile_down(w):
    wt = np.ascontiguousarray(w.T)
    t = wt.reshape(4, 16, 128, 16, 128).transpose(3, 0, 2, 1, 4)
    return np.ascontiguousarray(t).reshape(64 * 128, 16 * 128)


def _hadamard(n):
    h = np.array([[1.0]], dtype=np.float64)
    while h.shape[0] < n:
        h = np.block([[h, h], [h, -h]])
    return h


def _vec(v, nch):
    return np.ascontiguousarray(np.asarray(v, np.float32).reshape(nch, 128).T)


def kernel(L0_attn_x, L0_attn_Wq, L0_attn_bq, L0_attn_Wk, L0_attn_bk,
           L0_attn_Wv, L0_attn_bv, L0_attn_Wo, L0_attn_bo,
           L0_ffn_SU_ug, L0_ffn_SU_dp, L0_ffn_Wscale_ug, L0_ffn_Wscale_dp,
           L0_ffn_W_up, L0_ffn_W_gate, L0_ffn_W_down):
    x = np.ascontiguousarray(np.asarray(L0_attn_x, dtype=np.float32)).reshape(-1, D)
    n_tok = x.shape[0]
    T = n_tok // N_CORES

    ws = np.asarray(L0_ffn_Wscale_ug, np.float32)
    shared = dict(
        wv=np.ascontiguousarray(np.asarray(L0_attn_Wv, np.float32).T),
        wo=np.ascontiguousarray(np.asarray(L0_attn_Wo, np.float32).T),
        wup=_tile_upgate(np.asarray(L0_ffn_W_up, np.float32)),
        wgt=_tile_upgate(np.asarray(L0_ffn_W_gate, np.float32)),
        wdn=_tile_down(np.asarray(L0_ffn_W_down, np.float32)),
        bv=_vec(L0_attn_bv, NCD), bo=_vec(L0_attn_bo, NCD),
        suug=_vec(L0_ffn_SU_ug, NCD), sudp=_vec(L0_ffn_SU_dp, NCI),
        wsu=_vec(ws[:I], NCI), wsg=_vec(ws[I:], NCI),
        wsd=_vec(L0_ffn_Wscale_dp, NCD),
        h1c=(_hadamard(128) * FWHT_A).astype(np.float32),
        h2c=(_hadamard(128) * FWHT_A).astype(np.float32),
        identc=np.eye(128, dtype=np.float16),
    )

    nc = build_program(T)
    in_maps = []
    for ci in range(N_CORES):
        xs = x[ci * T:(ci + 1) * T]
        m = dict(shared)
        m["x_tm"] = np.ascontiguousarray(xs)
        m["x_fm"] = np.ascontiguousarray(xs.T)
        in_maps.append(m)

    res = run_bass_kernel_spmd(nc, in_maps, core_ids=list(range(N_CORES)))
    y = np.concatenate([res.results[ci]["y"] for ci in range(N_CORES)], axis=1)
    return np.ascontiguousarray(y.T).reshape(np.asarray(L0_attn_x).shape).astype(np.float32)


# version 23
# speedup vs baseline: 1.0402x; 1.0053x over previous
import numpy as np

import concourse.bass as bass
import concourse.mybir as mybir
from concourse.bass_utils import run_bass_kernel_spmd

F32 = mybir.dt.float32
F16 = mybir.dt.float16
AF = mybir.ActivationFunctionType
ALU = mybir.AluOpType
AX = mybir.AxisListType.X

D = 2048
H = 16
I = 8192
NCD = D // 128
NCI = I // 128
N_CORES = 8
EPS = 1e-5
FWHT_A = 0.0004873637808486819
FWHT_K1 = 45.34002685546875
FWHT_K2 = 22.670013427734375


class Ctr:
    def __init__(self, handle):
        self.h = handle
        self.n = 0

    def inc(self, ins, k=1):
        ins.then_inc(self.h, k)
        self.n += k
        return self.n


class Arena:

    def __init__(self, ap, ncols):
        self.ap = ap
        self.avail = [(0, ncols)]
        self.used = {}

    def alloc(self, name, n, high=False):
        idxs = range(len(self.avail) - 1, -1, -1) if high else range(len(self.avail))
        for i in idxs:
            s, l = self.avail[i]
            if l >= n:
                if high:
                    start = s + l - n
                    self.used[name] = (start, n)
                    if l == n:
                        self.avail.pop(i)
                    else:
                        self.avail[i] = (s, l - n)
                    return self.ap[:, start:start + n]
                self.used[name] = (s, n)
                if l == n:
                    self.avail.pop(i)
                else:
                    self.avail[i] = (s + n, l - n)
                return self.ap[:, s:s + n]
        raise MemoryError(f"arena: {name} needs {n} cols; free={self.avail}")

    def release(self, *names):
        for name in names:
            s, n = self.used.pop(name)
            self.avail.append((s, n))
        self.avail.sort()
        merged = []
        for s, l in self.avail:
            if merged and merged[-1][0] + merged[-1][1] == s:
                merged[-1] = (merged[-1][0], merged[-1][1] + l)
            else:
                merged.append((s, l))
        self.avail = merged


def build_program(T):
    NT = T // 128
    HW = min(512, T)
    NH = T // HW
    nc = bass.Bass("TRN2", target_bir_lowering=False, debug=False)

    x_tm_d = nc.dram_tensor("x_tm", [T, D], F32, kind="ExternalInput").ap()
    x_fm_d = nc.dram_tensor("x_fm", [D, T], F32, kind="ExternalInput").ap()
    wv_d = nc.dram_tensor("wv", [D, D], F32, kind="ExternalInput").ap()
    wo_d = nc.dram_tensor("wo", [D, D], F32, kind="ExternalInput").ap()
    wup_d = nc.dram_tensor("wup", [NCI * 128, NCD * 128], F32, kind="ExternalInput").ap()
    wgt_d = nc.dram_tensor("wgt", [NCI * 128, NCD * 128], F32, kind="ExternalInput").ap()
    wdn_d = nc.dram_tensor("wdn", [64 * 128, 16 * 128], F32, kind="ExternalInput").ap()
    vec_d = {}
    for nm, w in (("bv", NCD), ("bo", NCD), ("suug", NCD), ("sudp", NCI),
                  ("wsu", NCI), ("wsg", NCI), ("wsd", NCD)):
        vec_d[nm] = nc.dram_tensor(nm, [128, w], F32, kind="ExternalInput").ap()
    h1_d = nc.dram_tensor("h1c", [128, 128], F32, kind="ExternalInput").ap()
    h2_d = nc.dram_tensor("h2c", [128, 128], F32, kind="ExternalInput").ap()
    id_d = nc.dram_tensor("identc", [128, 128], F16, kind="ExternalInput").ap()
    y_d = nc.dram_tensor("y", [D, T], F32, kind="ExternalOutput").ap()
    scr_d = nc.dram_tensor("scr", [D, T], F32, kind="Internal").ap()

    A16 = 93 * 1024 + 512
    A32 = 4416

    live_ps = []

    def ps_alloc(name, shape, dt):
        cm = nc.psum_tensor(name, shape, dt)
        t = cm.__enter__()
        live_ps.append((name, cm))
        return t

    def ps_free(*names):
        for name in names:
            nm, cm = live_ps.pop()
            assert nm == name, f"psum free order: {nm} != {name}"
            cm.__exit__(None, None, None)

    from contextlib import ExitStack
    _es = ExitStack()

    def _sem(nm):
        return _es.enter_context(nc.semaphore(nm))

    with _es:
        arena16_t = _es.enter_context(nc.sbuf_tensor("arena16", [128, A16], F16))
        arena32_t = _es.enter_context(nc.sbuf_tensor("arena32", [128, A32], F32))
        hdp_t = _es.enter_context(nc.sbuf_tensor("hdp", [128, 128], F16))
        hdpn_t = _es.enter_context(nc.sbuf_tensor("hdpn", [128, 128], F16))
        ident_t = _es.enter_context(nc.sbuf_tensor("ident", [128, 128], F16))
        vecs_t = _es.enter_context(nc.sbuf_tensor("vecs", [128, NCD * 5 + NCI * 5 + 2], F32))
        stat_t = _es.enter_context(nc.sbuf_tensor("stat", [128, 64], F32))
        s_gp = _sem("s_gp")
        s_w0 = _sem("s_w0")
        s_w1 = _sem("s_w1")
        s_x0 = _sem("s_x0")
        s_x1 = _sem("s_x1")
        s_o0 = _sem("s_o0")
        s_o1 = _sem("s_o1")
        s_pe = _sem("s_pe")
        s_dv = _sem("s_dv")
        s_sc = _sem("s_sc")
        s_gs = _sem("s_gs")
        s_d0 = _sem("s_d0")
        s_d1 = _sem("s_d1")
        s_d2 = _sem("s_d2")
        block = _es.enter_context(nc.Block())
        a16 = Arena(arena16_t[:], A16)
        a32 = Arena(arena32_t[:], A32)
        hdp = hdp_t[:]
        hdpn = hdpn_t[:]
        ident = ident_t[:]
        stat = stat_t[:]
        vecs = {}
        off = 0
        for nm, w in (("bv", NCD), ("bo", NCD), ("suug", NCD), ("wsd", NCD),
                      ("sdpe", NCD), ("sudp", NCI), ("wsu", NCI), ("wsg", NCI),
                      ("supe", NCI), ("sgpe", NCI), ("bveff", 1), ("pad", 1)):
            vecs[nm] = vecs_t[:, off:off + w]
            off += w
        gp = Ctr(s_gp)
        sw = [Ctr(s_w0), Ctr(s_w1)]
        sx = [Ctr(s_x0), Ctr(s_x1)]
        so = [Ctr(s_o0), Ctr(s_o1)]
        pe = Ctr(s_pe)
        dv = Ctr(s_dv)
        sc = Ctr(s_sc)
        gs = Ctr(s_gs)
        sd = [Ctr(s_d0), Ctr(s_d1), Ctr(s_d2)]

        hsu = a16.alloc("hsu", NCD * 128)
        h1sb = a32.alloc("h1sb", 128)
        h2sb = a32.alloc("h2sb", 128)
        tmp64 = a32.alloc("tmp64", NCI)

        @block.gpsimd
        def _(g):
            for dst, src in ((ident, id_d), (h1sb, h1_d), (h2sb, h2_d)):
                gp.inc(g.dma_start(out=dst, in_=src[:]), 16)
            for nm in ("bv", "bo", "suug", "sudp", "wsu", "wsg", "wsd"):
                gp.inc(g.dma_start(out=vecs[nm], in_=vec_d[nm][:]), 16)
        const_dma = gp.n

        @block.vector
        def _(v):
            v.wait_ge(s_gp, const_dma)
            for c in range(NCD):
                dv.inc(v.tensor_scalar(hsu[:, c * 128:(c + 1) * 128], h1sb,
                                       vecs["suug"][:, c:c + 1], None, ALU.mult))
            dv.inc(v.tensor_copy(hdp, h2sb))
            dv.inc(v.tensor_scalar(hdpn, h2sb, -1.0, None, ALU.mult))
            dv.inc(v.tensor_scalar(tmp64, vecs["wsu"], FWHT_K1, None, ALU.mult))
            dv.inc(v.reduce_sum(vecs["bveff"], vecs["bv"], axis=AX))
            v.wait_ge(s_dv, dv.n)
            dv.inc(v.tensor_mul(vecs["supe"], tmp64, vecs["sudp"]))
            dv.inc(v.tensor_scalar(vecs["sgpe"], vecs["wsg"], FWHT_K1, None, ALU.mult))
            dv.inc(v.tensor_scalar(vecs["sdpe"], vecs["wsd"], FWHT_K2, None, ALU.mult))
        const_dv = dv.n

        xn = a16.alloc("xn", NCD * T)
        xn_tm_b = [a16.alloc(f"xn_tm{i}", D) for i in range(2)]
        xt_b = [a32.alloc(f"xt{i}", D) for i in range(2)]
        veff = a16.alloc("veff", NCD * 128)
        woeff = [a16.alloc(f"woe{i}", D) for i in range(2)]
        vwsl = [a16.alloc(f"vwsl{i}", 2 * D) for i in range(2)]
        vtmp = a16.alloc("vtmp", 3584)
        pstr = [ps_alloc(f"pstr{i}", [128, 1024], F16) for i in range(2)]

        NPAIR = NCD // 2
        NFP = max(1, NPAIR // NT)
        assert NFP * NT == NPAIR or NT == 1
        ln_sc_done = 2 * NT
        ln_pe_done = 2 * NT

        sqrt_sc = [0] * NT
        evac1_sc = [0] * NT
        _scpos = 0
        for _t in range(NT):
            _scpos += 1
            sqrt_sc[_t] = _scpos
            if _t >= 1:
                _scpos += 1
                evac1_sc[_t - 1] = _scpos
        _scpos += 1
        evac1_sc[NT - 1] = _scpos

        eps_mark = [0] * NT
        apply_mark = [0] * NT
        woadd_mark = [0] * NPAIR
        _dvpos = const_dv
        for _t in range(NT + 1):
            if _t < NT:
                _dvpos += 6
                eps_mark[_t] = _dvpos
            if _t >= 1:
                _tt = _t - 1
                _dvpos += 3
                apply_mark[_tt] = _dvpos
                _dvpos += 1
                for _jf in range(NFP):
                    _p = _tt * NFP + _jf
                    if _p < NPAIR:
                        _dvpos += 2
                        woadd_mark[_p] = _dvpos
        ln_dv_done = _dvpos

        wo_final = woeff[(NPAIR * 2 - 1) % 2]
        vw_mark = []

        @block.gpsimd
        def _(g):
            for p in range(NPAIR):
                if p >= 1:
                    g.wait_ge(s_gs, 4 * p)
                    g.wait_ge(s_dv, woadd_mark[p - 1])
                for half in range(2):
                    k = 2 * p + half
                    sw[0].inc(g.dma_start(out=vwsl[0][:, half * D:(half + 1) * D],
                                          in_=wv_d[k * 128:(k + 1) * 128, :]), 16)
                    sw[1].inc(g.dma_start(out=vwsl[1][:, half * D:(half + 1) * D],
                                          in_=wo_d[k * 128:(k + 1) * 128, :]), 16)
                vw_mark.append((sw[0].n, sw[1].n))
                g.wait_ge(s_w0, sw[0].n)
                v0 = vwsl[0].rearrange("p (b a two d) -> p b a two d",
                                       b=2, two=2, d=128)
                vt1 = vtmp[:, 0:2048].rearrange("p (b a d) -> p b a d", b=2, d=128)
                gs.inc(g.tensor_add(vt1, v0[:, :, :, 0, :], v0[:, :, :, 1, :]))
                g.wait_ge(s_gs, gs.n)
                t1f = vtmp[:, 0:2048].rearrange("p (b a two d) -> p b a two d",
                                                b=2, two=2, d=128)
                vt2 = vtmp[:, 2048:3072].rearrange("p (b a d) -> p b a d", b=2, d=128)
                gs.inc(g.tensor_add(vt2, t1f[:, :, :, 0, :], t1f[:, :, :, 1, :]))
                g.wait_ge(s_gs, gs.n)
                t2f = vtmp[:, 2048:3072].rearrange("p (b a two d) -> p b a two d",
                                                   b=2, two=2, d=128)
                vt3 = vtmp[:, 3072:3584].rearrange("p (b a d) -> p b a d", b=2, d=128)
                gs.inc(g.tensor_add(vt3, t2f[:, :, :, 0, :], t2f[:, :, :, 1, :]))
                g.wait_ge(s_gs, gs.n)
                t3f = vtmp[:, 3072:3584].rearrange("p (b two d) -> p b two d",
                                                   b=2, d=128)
                gs.inc(g.tensor_add(
                    veff[:, 2 * p * 128:(2 * p + 2) * 128].rearrange(
                        "p (b one d) -> p b one d", one=1, d=128),
                    t3f[:, :, 0:1, :], t3f[:, :, 1:2, :]))
        veff_done = gs.n
        assert veff_done == 4 * NPAIR

        xt_mark = []

        @block.sync
        def _(s):
            for t in range(NT):
                if t >= 2:
                    s.wait_ge(s_dv, apply_mark[t - 2])
                sx[t % 2].inc(s.dma_start(out=xt_b[t % 2],
                                          in_=x_tm_d[t * 128:(t + 1) * 128, :]), 16)
                xt_mark.append((sx[t % 2].h, sx[t % 2].n))

        @block.scalar
        def _(s):
            def evac1(t):
                s.wait_ge(s_pe, 2 * (t + 1))
                dst = xn.rearrange("p (c t2) -> p c t2", c=NCD)[
                    :, 8:16, t * 128:(t + 1) * 128]
                sc.inc(s.activation(dst, pstr[1][:].rearrange(
                    "p (c t2) -> p c t2", c=8), AF.Identity))
            for t in range(NT):
                s.wait_ge(s_dv, eps_mark[t])
                sc.inc(s.activation(stat[:, 5 * t + 4:5 * t + 5],
                                    stat[:, 5 * t + 1:5 * t + 2], AF.Sqrt))
                if t >= 1:
                    evac1(t - 1)
            evac1(NT - 1)
        assert sc.n == ln_sc_done

        @block.vector
        def _(v):
            NB = D // 512
            for t in range(NT + 1):
                if t < NT:
                    v.wait_ge(*xt_mark[t])
                    for q in range(NB):
                        dv.inc(v.bn_stats(stat[:, 40 + q * 6:40 + (q + 1) * 6],
                                          xt_b[t % 2][:, q * 512:(q + 1) * 512]))
                    v.wait_ge(s_dv, dv.n)
                    dv.inc(v.bn_aggr(stat[:, 5 * t:5 * t + 2], stat[:, 40:40 + 6 * NB]))
                    v.wait_ge(s_dv, dv.n)
                    dv.inc(v.tensor_scalar(stat[:, 5 * t + 1:5 * t + 2],
                                           stat[:, 5 * t + 1:5 * t + 2], EPS, None, ALU.add))
                    assert dv.n == eps_mark[t]
                if t >= 1:
                    tt = t - 1
                    v.wait_ge(s_sc, sqrt_sc[tt])
                    dv.inc(v.reciprocal(stat[:, 5 * tt + 2:5 * tt + 3],
                                        stat[:, 5 * tt + 4:5 * tt + 5]))
                    v.wait_ge(s_dv, dv.n)
                    dv.inc(v.scalar_tensor_tensor(stat[:, 5 * tt + 3:5 * tt + 4],
                                                  stat[:, 5 * tt:5 * tt + 1], -1.0,
                                                  stat[:, 5 * tt + 2:5 * tt + 3],
                                                  ALU.mult, ALU.mult))
                    v.wait_ge(s_dv, dv.n)
                    dv.inc(v.tensor_scalar(xn_tm_b[tt % 2], xt_b[tt % 2],
                                           stat[:, 5 * tt + 2:5 * tt + 3],
                                           stat[:, 5 * tt + 3:5 * tt + 4],
                                           ALU.mult, ALU.add))
                    assert dv.n == apply_mark[tt]
                    v.wait_ge(s_pe, 2 * (tt + 1))
                    dst = xn.rearrange("p (c t2) -> p c t2", c=NCD)[
                        :, 0:8, tt * 128:(tt + 1) * 128]
                    dv.inc(v.tensor_copy(dst, pstr[0][:].rearrange(
                        "p (c t2) -> p c t2", c=8)))
                    for jf in range(NFP):
                        p = tt * NFP + jf
                        if p >= NPAIR:
                            continue
                        v.wait_ge(s_w1, vw_mark[p][1])
                        for half in range(2):
                            k = 2 * p + half
                            if k == 0:
                                dv.inc(v.tensor_copy(woeff[0], vwsl[1][:, 0:D]))
                            else:
                                v.wait_ge(s_dv, dv.n)
                                dv.inc(v.tensor_add(woeff[k % 2], woeff[(k - 1) % 2],
                                                    vwsl[1][:, half * D:(half + 1) * D]))
                        assert dv.n == woadd_mark[p]
        assert dv.n == ln_dv_done

        @block.tensor
        def _(te):
            for t in range(NT):
                te.wait_ge(s_dv, apply_mark[t])
                if t >= 1:
                    te.wait_ge(s_sc, evac1_sc[t - 1])
                for gb in range(2):
                    for c in range(8):
                        cc = gb * 8 + c
                        ins = te.transpose(pstr[gb][:, c * 128:(c + 1) * 128],
                                           xn_tm_b[t % 2][:, cc * 128:(cc + 1) * 128],
                                           ident)
                    pe.inc(ins)
        assert pe.n == ln_pe_done and dv.n == ln_dv_done and sc.n == ln_sc_done

        vsum = a16.alloc("vsum", T)
        a16.release("xn_tm0", "xn_tm1")
        a32.release("xt0", "xt1", "h1sb", "h2sb", "tmp64")
        ps_free("pstr1", "pstr0")
        psV = ps_alloc("psV", [128, T], F32)
        psO = [ps_alloc(f"psO{i}", [128, T], F32) for i in range(2)]
        psQx = ps_alloc("psQx", [128, T], F32)
        xfb = [a32.alloc(f"xfb{i}", T) for i in range(2)]
        accb = [a32.alloc(f"acb{i}", T) for i in range(2)]
        a16.release("vwsl0", "vwsl1", "vtmp")
        x1 = [a16.alloc(f"x1_{i}", T, high=True) for i in range(NCD)]
        x1 = x1 + [a16.alloc(f"sp{i}", T, high=True) for i in range(2)]

        vs_pe_done = ln_pe_done + 1
        vs_dv_done = ln_dv_done + 1
        f1_pe_done = vs_pe_done + NCD
        wo_pe_done = f1_pe_done + NCD
        f1_ev = []
        _dvx, _scx = vs_dv_done, ln_sc_done
        for c in range(NCD):
            if c % 2 == 0:
                _dvx += 1
                f1_ev.append((s_dv, _dvx))
            else:
                _scx += 1
                f1_ev.append((s_sc, _scx))
        f1_dv_done, f1_sc_done = _dvx, _scx
        f1b_dv_done = f1_dv_done + 4 * NCD

        def wo_acc_mark(m):
            return f1b_dv_done + m + 1

        wo_dv_done = f1b_dv_done + NCD

        @block.tensor
        def _(te):
            te.wait_ge(s_dv, ln_dv_done)
            te.wait_ge(s_sc, ln_sc_done)
            te.wait_ge(s_gs, veff_done)
            for hf in range(NH):
                for k in range(NCD):
                    ins = te.matmul(psV[:, hf * HW:(hf + 1) * HW],
                                    veff[:, k * 128:(k + 1) * 128],
                                    xn[:, k * T + hf * HW: k * T + (hf + 1) * HW],
                                    start=(k == 0), stop=(k == NCD - 1))
            pe.inc(ins)
            x1ps = [psQx, psV, psO[0], psO[1]]
            for c in range(NCD):
                if c == 1:
                    te.wait_ge(s_dv, vs_dv_done)
                if c >= 4:
                    te.wait_ge(f1_ev[c - 4][0], f1_ev[c - 4][1])
                dstp = x1ps[c % 4]
                for hf in range(NH):
                    ins = te.matmul(dstp[:, hf * HW:(hf + 1) * HW],
                                    hsu[:, c * 128:(c + 1) * 128],
                                    xn[:, c * T + hf * HW: c * T + (hf + 1) * HW],
                                    start=True, stop=True)
                pe.inc(ins)
            te.wait_ge(s_dv, vs_dv_done)
            for m in range(NCD):
                par = m % 2
                if m == 0:
                    te.wait_ge(*f1_ev[NCD - 2])
                    te.wait_ge(*f1_ev[NCD - 1])
                if m >= 2:
                    te.wait_ge(s_dv, wo_acc_mark(m - 2))
                for hf in range(NH):
                    ins = te.matmul(psO[par][:, hf * HW:(hf + 1) * HW],
                                    wo_final[:, m * 128:(m + 1) * 128],
                                    vsum[:, hf * HW:(hf + 1) * HW],
                                    start=True, stop=True)
                pe.inc(ins)
        assert pe.n == wo_pe_done

        @block.scalar
        def _(s):
            for c in range(1, NCD, 2):
                s.wait_ge(s_pe, vs_pe_done + c + 1)
                sc.inc(s.activation(x1[c], (psV if c % 4 == 1 else psO[1])[:],
                                    AF.Identity))
        assert sc.n == f1_sc_done

        x1_slots = list(range(NCD))
        x1_final_spare = []

        wo_xf_dma = []
        wo_st_mark = []

        @block.sync
        def _(s):
            s.wait_ge(s_dv, vs_dv_done)
            s.wait_ge(s_sc, ln_sc_done)
            for m in range(NCD):
                if m >= 2:
                    s.wait_ge(s_dv, wo_acc_mark(m - 2))
                sx[m % 2].inc(s.dma_start(out=xfb[m % 2],
                                          in_=x_fm_d[m * 128:(m + 1) * 128, :]), 16)
                wo_xf_dma.append((sx[m % 2].h, sx[m % 2].n))
                s.wait_ge(s_dv, wo_acc_mark(m))
                so[m % 2].inc(s.dma_start(out=scr_d[m * 128:(m + 1) * 128, :],
                                          in_=accb[m % 2]), 16)
                wo_st_mark.append((so[m % 2].h, so[m % 2].n))
        wo_so_done = (so[0].n, so[1].n)

        @block.vector
        def _(v):
            v.wait_ge(s_pe, vs_pe_done)
            dv.inc(v.tensor_scalar(vsum, psV[:], vecs["bveff"], None, ALU.add))
            for c in range(0, NCD, 2):
                v.wait_ge(s_pe, vs_pe_done + c + 1)
                dv.inc(v.tensor_copy(x1[c], (psQx if c % 4 == 0
                                             else psO[0])[:]))
            assert dv.n == f1_dv_done
            v.wait_ge(s_sc, f1_sc_done)
            spare = [NCD, NCD + 1]
            for st in range(4):
                step = 1 << st
                for c in range(NCD):
                    if c & step:
                        continue
                    a, b = x1_slots[c], x1_slots[c | step]
                    fa, fb = spare
                    v.wait_ge(s_dv, dv.n)
                    dv.inc(v.tensor_add(x1[fa], x1[a], x1[b]))
                    dv.inc(v.tensor_sub(x1[fb], x1[a], x1[b]))
                    spare = [a, b]
                    x1_slots[c], x1_slots[c | step] = fa, fb
            x1_final_spare.extend(spare)
            assert dv.n == f1b_dv_done
            for m in range(NCD):
                par = m % 2
                v.wait_ge(s_pe, f1_pe_done + m + 1)
                v.wait_ge(*wo_xf_dma[m])
                if m >= 2:
                    v.wait_ge(*wo_st_mark[m - 2])
                dv.inc(v.scalar_tensor_tensor(accb[par],
                                              psO[par][:],
                                              vecs["bo"][:, m:m + 1],
                                              xfb[par], ALU.add, ALU.add))
            assert dv.n == wo_dv_done


        x1_free_names = [(f"x1_{ix}" if ix < NCD else f"sp{ix - NCD}")
                         for ix in x1_final_spare]

        a16.release("xn", "hsu", "veff", "woe0", "woe1", "vsum", *x1_free_names)
        hbuf = [a16.alloc(f"h_{i}", T, high=True) for i in range(NCI)]
        hbuf = hbuf + [a16.alloc(f"hsp{i}", T, high=True) for i in range(2)]
        uslab = [a16.alloc(f"usl{i}", NCD * 128) for i in range(2)]
        gslab = [a16.alloc(f"gsl{i}", NCD * 128) for i in range(2)]
        sgb = [a16.alloc(f"sgb{i}", T, high=True) for i in range(2)]
        ps_free("psQx", "psO1", "psO0", "psV")
        psU = [ps_alloc(f"psU{i}", [128, 2 * T], F32) for i in range(2)]

        ug_slab_dma = []

        bfly_sched = {i: [] for i in range(NCI)}
        for s5 in range(5):
            step5 = 1 << s5
            for c5 in range(NCI):
                if c5 & step5:
                    continue
                bfly_sched[c5 | (2 * step5 - 1)].append((s5, c5))
        hmul_mark = []
        hmul_only = []
        _m = wo_dv_done
        for i in range(NCI):
            _m += 1
            hmul_only.append(_m)
            _m += 2 * len(bfly_sched[i])
            hmul_mark.append(_m)

        def ug_silu(i):
            return f1_sc_done + i + 1

        upg_base = wo_pe_done

        ug_dv_done = hmul_mark[-1]
        ug_sc_done = f1_sc_done + NCI
        ug_pe_done = wo_pe_done + NCI

        @block.gpsimd
        def _(g):
            g.wait_ge(s_pe, wo_pe_done)
            g.wait_ge(s_dv, wo_dv_done)
            for i in range(NCI):
                if i >= 2:
                    g.wait_ge(s_pe, wo_pe_done + i - 1)
                sw[i % 2].inc(g.dma_start(out=uslab[i % 2],
                                          in_=wup_d[i * 128:(i + 1) * 128, :]), 16)
                sw[i % 2].inc(g.dma_start(out=gslab[i % 2],
                                          in_=wgt_d[i * 128:(i + 1) * 128, :]), 16)
                ug_slab_dma.append((sw[i % 2].h, sw[i % 2].n))

        @block.tensor
        def _(te):
            te.wait_ge(s_dv, wo_dv_done)
            te.wait_ge(s_sc, f1_sc_done)
            for i in range(NCI):
                par = i % 2
                te.wait_ge(*ug_slab_dma[i])
                if i >= 2:
                    te.wait_ge(s_sc, ug_silu(i - 2))
                    te.wait_ge(s_dv, hmul_only[i - 2])
                for hf in range(NH):
                    for k in range(NCD):
                        te.matmul(psU[par][:, hf * HW:(hf + 1) * HW],
                                  uslab[par][:, k * 128:(k + 1) * 128],
                                  x1[x1_slots[k]][:, hf * HW:(hf + 1) * HW],
                                  start=(k == 0), stop=(k == NCD - 1))
                for hf in range(NH):
                    for k in range(NCD):
                        ins = te.matmul(psU[par][:, T + hf * HW: T + (hf + 1) * HW],
                                        gslab[par][:, k * 128:(k + 1) * 128],
                                        x1[x1_slots[k]][:, hf * HW:(hf + 1) * HW],
                                        start=(k == 0), stop=(k == NCD - 1))
                pe.inc(ins)

        @block.scalar
        def _(s):
            for i in range(NCI):
                par = i % 2
                s.wait_ge(s_pe, upg_base + i + 1)
                if i >= 2:
                    s.wait_ge(s_dv, hmul_only[i - 2])
                sc.inc(s.activation(sgb[par], psU[par][:, T:2 * T], AF.Silu,
                                    scale=vecs["sgpe"][:, i:i + 1]))

        h_slots = list(range(NCI))
        h_level = [0] * NCI
        cross_order = []

        @block.vector
        def _(v):
            spare = [NCI, NCI + 1]
            for i in range(NCI):
                par = i % 2
                v.wait_ge(s_pe, upg_base + i + 1)
                v.wait_ge(s_sc, ug_silu(i))
                dv.inc(v.scalar_tensor_tensor(hbuf[i], psU[par][:, 0:T],
                                              vecs["supe"][:, i:i + 1],
                                              sgb[par], ALU.mult, ALU.mult))
                for (s5, c5) in bfly_sched[i]:
                    d5 = c5 | (1 << s5)
                    a, b = h_slots[c5], h_slots[d5]
                    fa, fb = spare
                    v.wait_ge(s_dv, dv.n)
                    dv.inc(v.tensor_add(hbuf[fa], hbuf[a], hbuf[b]))
                    dv.inc(v.tensor_sub(hbuf[fb], hbuf[a], hbuf[b]))
                    spare = [a, b]
                    h_slots[c5], h_slots[d5] = fa, fb
                    h_level[c5] += 1
                    h_level[d5] += 1
                    for cc in (c5, d5):
                        if h_level[cc] == 5:
                            cross_order.append((cc, dv.n))
                assert dv.n == hmul_mark[i]
        assert dv.n == ug_dv_done and sc.n == ug_sc_done and pe.n == ug_pe_done
        assert len(cross_order) == NCI and all(l == 5 for l in h_level)

        for nm6 in ([f"x1_{i2}" for i2 in range(NCD)] + ["sp0", "sp1"]):
            if nm6 not in x1_free_names:
                a16.release(nm6)

        f2_ev = []
        _dvx, _scx = ug_dv_done, ug_sc_done
        for c in range(NCI):
            if c % 2 == 0:
                _dvx += 1
            else:
                _scx += 1
            f2_ev.append((_dvx, _scx))
        f2_dv_done, f2_sc_done = _dvx, _scx
        f2_pe_done = ug_pe_done + NCI

        F2DEPTH = 4 if T >= 512 else 2

        def psq4(j):
            if F2DEPTH == 4:
                return psU[j % 2][:, ((j // 2) % 2) * T:(((j // 2) % 2) + 1) * T]
            return psU[j % 2][:, 0:T]

        cmark = {c: mk for c, mk in cross_order}
        pairs = sorted(range(NCI // 2), key=lambda p: max(cmark[p], cmark[p + 32]))
        out_chunks = []
        for p in pairs:
            out_chunks.append((p, max(cmark[p], cmark[p + 32]), +1))
            out_chunks.append((p + 32, max(cmark[p], cmark[p + 32]), -1))

        @block.tensor
        def _(te):
            te.wait_ge(s_sc, ug_sc_done)
            for j in range(NCI):
                c, mk, sgn = out_chunks[j]
                p = c % 32
                te.wait_ge(s_dv, mk)
                if j >= F2DEPTH:
                    te.wait_ge(s_dv, f2_ev[j - F2DEPTH][0])
                    te.wait_ge(s_sc, f2_ev[j - F2DEPTH][1])
                dst = psq4(j)
                for hf in range(NH):
                    te.matmul(dst[:, hf * HW:(hf + 1) * HW],
                              hdp,
                              hbuf[h_slots[p]][:, hf * HW:(hf + 1) * HW],
                              start=True, stop=False)
                    ins = te.matmul(dst[:, hf * HW:(hf + 1) * HW],
                                    hdp if sgn > 0 else hdpn,
                                    hbuf[h_slots[p + 32]][:, hf * HW:(hf + 1) * HW],
                                    start=False, stop=True)
                pe.inc(ins)

        @block.vector
        def _(v):
            for j in range(0, NCI, 2):
                v.wait_ge(s_pe, ug_pe_done + (j | 1) + 1)
                dv.inc(v.tensor_copy(hbuf[h_slots[out_chunks[j][0]]], psq4(j)))

        @block.scalar
        def _(s):
            for j in range(1, NCI, 2):
                s.wait_ge(s_pe, ug_pe_done + j + 1)
                sc.inc(s.activation(hbuf[h_slots[out_chunks[j][0]]], psq4(j), AF.Identity))
        assert dv.n == f2_dv_done and sc.n == f2_sc_done and pe.n == f2_pe_done
        f2b_dv_done = f2_dv_done

        a16.release("usl0", "usl1", "gsl0", "gsl1", "sgb0", "sgb1")
        a32.release("xfb0", "xfb1", "acb0", "acb1")
        dslab = [a16.alloc(f"dsl{i}", 16 * 128) for i in range(12)]
        xfb2 = [a32.alloc(f"xf2{i}", T) for i in range(2)]
        outb = [a32.alloc(f"out{i}", T) for i in range(2)]

        dn_slab_dma = []
        dn_acc_dma = []
        dn_st_mark = []

        def dn_fin(m):
            return f2b_dv_done + m + 1

        @block.gpsimd
        def _(g):
            g.wait_ge(s_dv, ug_dv_done)
            g.wait_ge(s_pe, ug_pe_done)
            for m in range(NCD):
                if m >= 3:
                    g.wait_ge(s_pe, f2_pe_done + m - 2)
                for kq in range(4):
                    idx = m * 4 + kq
                    sd[m % 3].inc(g.dma_start(
                        out=dslab[(m % 3) * 4 + kq],
                        in_=wdn_d[idx * 128:(idx + 1) * 128, :]), 16)
                dn_slab_dma.append((sd[m % 3].h, sd[m % 3].n))

        @block.sync
        def _(s):
            s.wait_ge(s_dv, f2b_dv_done)
            s.wait_ge(s_sc, f2_sc_done)
            for m in range(NCD):
                if m >= 2:
                    s.wait_ge(s_dv, dn_fin(m - 2))
                sx[m % 2].inc(s.dma_start(out=xfb2[m % 2],
                                          in_=scr_d[m * 128:(m + 1) * 128, :]), 16)
                dn_acc_dma.append((sx[m % 2].h, sx[m % 2].n))
                s.wait_ge(s_dv, dn_fin(m))
                so[m % 2].inc(s.dma_start(out=y_d[m * 128:(m + 1) * 128, :],
                                          in_=outb[m % 2]), 16)
                dn_st_mark.append((so[m % 2].h, so[m % 2].n))
            s.wait_ge(s_o0, so[0].n)
            s.wait_ge(s_o1, so[1].n)

        @block.tensor
        def _(te):
            te.wait_ge(s_dv, f2b_dv_done)
            te.wait_ge(s_sc, f2_sc_done)
            for m in range(NCD):
                par = m % 2
                if m >= 2:
                    te.wait_ge(s_dv, dn_fin(m - 2))
                te.wait_ge(*dn_slab_dma[m])
                for hf in range(NH):
                    for kq in range(4):
                        for k in range(16):
                            kk = kq * 16 + k
                            ins = te.matmul(
                                psU[par][:, hf * HW:(hf + 1) * HW],
                                dslab[(m % 3) * 4 + kq][:, k * 128:(k + 1) * 128],
                                hbuf[h_slots[kk]][:, hf * HW:(hf + 1) * HW],
                                start=(kq == 0 and k == 0), stop=(kq == 3 and k == 15))
                pe.inc(ins)

        @block.vector
        def _(v):
            v.wait_ge(s_o0, wo_so_done[0])
            v.wait_ge(s_o1, wo_so_done[1])
            for m in range(NCD):
                par = m % 2
                v.wait_ge(s_pe, f2_pe_done + m + 1)
                v.wait_ge(*dn_acc_dma[m])
                if m >= 2:
                    v.wait_ge(*dn_st_mark[m - 2])
                dv.inc(v.scalar_tensor_tensor(outb[m % 2],
                                              psU[par][:, 0:T],
                                              vecs["sdpe"][:, m:m + 1],
                                              xfb2[m % 2], ALU.mult, ALU.add))
        ps_free("psU1", "psU0")

    return nc


def _tile_upgate(w):
    wt = np.ascontiguousarray(w.T)
    t = wt.reshape(16, 128, 64, 128).transpose(2, 1, 0, 3)
    return np.ascontiguousarray(t).reshape(64 * 128, 16 * 128)


def _tile_down(w):
    wt = np.ascontiguousarray(w.T)
    t = wt.reshape(4, 16, 128, 16, 128).transpose(3, 0, 2, 1, 4)
    return np.ascontiguousarray(t).reshape(64 * 128, 16 * 128)


def _hadamard(n):
    h = np.array([[1.0]], dtype=np.float64)
    while h.shape[0] < n:
        h = np.block([[h, h], [h, -h]])
    return h


def _vec(v, nch):
    return np.ascontiguousarray(np.asarray(v, np.float32).reshape(nch, 128).T)


def kernel(L0_attn_x, L0_attn_Wq, L0_attn_bq, L0_attn_Wk, L0_attn_bk,
           L0_attn_Wv, L0_attn_bv, L0_attn_Wo, L0_attn_bo,
           L0_ffn_SU_ug, L0_ffn_SU_dp, L0_ffn_Wscale_ug, L0_ffn_Wscale_dp,
           L0_ffn_W_up, L0_ffn_W_gate, L0_ffn_W_down):
    x = np.ascontiguousarray(np.asarray(L0_attn_x, dtype=np.float32)).reshape(-1, D)
    n_tok = x.shape[0]
    T = n_tok // N_CORES

    ws = np.asarray(L0_ffn_Wscale_ug, np.float32)
    shared = dict(
        wv=np.ascontiguousarray(np.asarray(L0_attn_Wv, np.float32).T),
        wo=np.ascontiguousarray(np.asarray(L0_attn_Wo, np.float32).T),
        wup=_tile_upgate(np.asarray(L0_ffn_W_up, np.float32)),
        wgt=_tile_upgate(np.asarray(L0_ffn_W_gate, np.float32)),
        wdn=_tile_down(np.asarray(L0_ffn_W_down, np.float32)),
        bv=_vec(L0_attn_bv, NCD), bo=_vec(L0_attn_bo, NCD),
        suug=_vec(L0_ffn_SU_ug, NCD), sudp=_vec(L0_ffn_SU_dp, NCI),
        wsu=_vec(ws[:I], NCI), wsg=_vec(ws[I:], NCI),
        wsd=_vec(L0_ffn_Wscale_dp, NCD),
        h1c=(_hadamard(128) * FWHT_A).astype(np.float32),
        h2c=(_hadamard(128) * FWHT_A).astype(np.float32),
        identc=np.eye(128, dtype=np.float16),
    )

    nc = build_program(T)
    in_maps = []
    for ci in range(N_CORES):
        xs = x[ci * T:(ci + 1) * T]
        m = dict(shared)
        m["x_tm"] = np.ascontiguousarray(xs)
        m["x_fm"] = np.ascontiguousarray(xs.T)
        in_maps.append(m)

    res = run_bass_kernel_spmd(nc, in_maps, core_ids=list(range(N_CORES)))
    y = np.concatenate([res.results[ci]["y"] for ci in range(N_CORES)], axis=1)
    return np.ascontiguousarray(y.T).reshape(np.asarray(L0_attn_x).shape).astype(np.float32)


# version 24
# speedup vs baseline: 1.0562x; 1.0154x over previous
import numpy as np

import concourse.bass as bass
import concourse.mybir as mybir
from concourse.bass_utils import run_bass_kernel_spmd

F32 = mybir.dt.float32
F16 = mybir.dt.float16
AF = mybir.ActivationFunctionType
ALU = mybir.AluOpType
AX = mybir.AxisListType.X

D = 2048
H = 16
I = 8192
NCD = D // 128
NCI = I // 128
N_CORES = 8
EPS = 1e-5
FWHT_A = 0.0004873637808486819
FWHT_K1 = 45.34002685546875
FWHT_K2 = 22.670013427734375


class Ctr:
    def __init__(self, handle):
        self.h = handle
        self.n = 0

    def inc(self, ins, k=1):
        ins.then_inc(self.h, k)
        self.n += k
        return self.n


class Arena:

    def __init__(self, ap, ncols):
        self.ap = ap
        self.avail = [(0, ncols)]
        self.used = {}

    def alloc(self, name, n, high=False):
        idxs = range(len(self.avail) - 1, -1, -1) if high else range(len(self.avail))
        for i in idxs:
            s, l = self.avail[i]
            if l >= n:
                if high:
                    start = s + l - n
                    self.used[name] = (start, n)
                    if l == n:
                        self.avail.pop(i)
                    else:
                        self.avail[i] = (s, l - n)
                    return self.ap[:, start:start + n]
                self.used[name] = (s, n)
                if l == n:
                    self.avail.pop(i)
                else:
                    self.avail[i] = (s + n, l - n)
                return self.ap[:, s:s + n]
        raise MemoryError(f"arena: {name} needs {n} cols; free={self.avail}")

    def release(self, *names):
        for name in names:
            s, n = self.used.pop(name)
            self.avail.append((s, n))
        self.avail.sort()
        merged = []
        for s, l in self.avail:
            if merged and merged[-1][0] + merged[-1][1] == s:
                merged[-1] = (merged[-1][0], merged[-1][1] + l)
            else:
                merged.append((s, l))
        self.avail = merged


def build_program(T):
    NT = T // 128
    HW = min(512, T)
    NH = T // HW
    nc = bass.Bass("TRN2", target_bir_lowering=False, debug=False)

    x_tm_d = nc.dram_tensor("x_tm", [T, D], F32, kind="ExternalInput").ap()
    x_fm_d = nc.dram_tensor("x_fm", [D, T], F32, kind="ExternalInput").ap()
    wv_d = nc.dram_tensor("wv", [D, D], F32, kind="ExternalInput").ap()
    wo_d = nc.dram_tensor("wo", [D, D], F32, kind="ExternalInput").ap()
    wup_d = nc.dram_tensor("wup", [NCI * 128, NCD * 128], F32, kind="ExternalInput").ap()
    wgt_d = nc.dram_tensor("wgt", [NCI * 128, NCD * 128], F32, kind="ExternalInput").ap()
    wdn_d = nc.dram_tensor("wdn", [64 * 128, 16 * 128], F32, kind="ExternalInput").ap()
    vec_d = {}
    for nm, w in (("bv", NCD), ("bo", NCD), ("suug", NCD), ("sudp", NCI),
                  ("wsu", NCI), ("wsg", NCI), ("wsd", NCD)):
        vec_d[nm] = nc.dram_tensor(nm, [128, w], F32, kind="ExternalInput").ap()
    h1_d = nc.dram_tensor("h1c", [128, 128], F32, kind="ExternalInput").ap()
    h2_d = nc.dram_tensor("h2c", [128, 128], F32, kind="ExternalInput").ap()
    id_d = nc.dram_tensor("identc", [128, 128], F16, kind="ExternalInput").ap()
    y_d = nc.dram_tensor("y", [D, T], F32, kind="ExternalOutput").ap()
    scr_d = nc.dram_tensor("scr", [D, T], F32, kind="Internal").ap()

    A16 = 93 * 1024 + 512
    A32 = 4416

    live_ps = []

    def ps_alloc(name, shape, dt):
        cm = nc.psum_tensor(name, shape, dt)
        t = cm.__enter__()
        live_ps.append((name, cm))
        return t

    def ps_free(*names):
        for name in names:
            nm, cm = live_ps.pop()
            assert nm == name, f"psum free order: {nm} != {name}"
            cm.__exit__(None, None, None)

    from contextlib import ExitStack
    _es = ExitStack()

    def _sem(nm):
        return _es.enter_context(nc.semaphore(nm))

    with _es:
        arena16_t = _es.enter_context(nc.sbuf_tensor("arena16", [128, A16], F16))
        arena32_t = _es.enter_context(nc.sbuf_tensor("arena32", [128, A32], F32))
        hdp_t = _es.enter_context(nc.sbuf_tensor("hdp", [128, 128], F16))
        hdpn_t = _es.enter_context(nc.sbuf_tensor("hdpn", [128, 128], F16))
        ident_t = _es.enter_context(nc.sbuf_tensor("ident", [128, 128], F16))
        vecs_t = _es.enter_context(nc.sbuf_tensor("vecs", [128, NCD * 5 + NCI * 5 + 2], F32))
        stat_t = _es.enter_context(nc.sbuf_tensor("stat", [128, 64], F32))
        s_gp = _sem("s_gp")
        s_w0 = _sem("s_w0")
        s_w1 = _sem("s_w1")
        s_x0 = _sem("s_x0")
        s_x1 = _sem("s_x1")
        s_o0 = _sem("s_o0")
        s_o1 = _sem("s_o1")
        s_pe = _sem("s_pe")
        s_dv = _sem("s_dv")
        s_sc = _sem("s_sc")
        s_gs = _sem("s_gs")
        s_d0 = _sem("s_d0")
        s_d1 = _sem("s_d1")
        s_d2 = _sem("s_d2")
        block = _es.enter_context(nc.Block())
        a16 = Arena(arena16_t[:], A16)
        a32 = Arena(arena32_t[:], A32)
        hdp = hdp_t[:]
        hdpn = hdpn_t[:]
        ident = ident_t[:]
        stat = stat_t[:]
        vecs = {}
        off = 0
        for nm, w in (("bv", NCD), ("bo", NCD), ("suug", NCD), ("wsd", NCD),
                      ("sdpe", NCD), ("sudp", NCI), ("wsu", NCI), ("wsg", NCI),
                      ("supe", NCI), ("sgpe", NCI), ("bveff", 1), ("pad", 1)):
            vecs[nm] = vecs_t[:, off:off + w]
            off += w
        gp = Ctr(s_gp)
        sw = [Ctr(s_w0), Ctr(s_w1)]
        sx = [Ctr(s_x0), Ctr(s_x1)]
        so = [Ctr(s_o0), Ctr(s_o1)]
        pe = Ctr(s_pe)
        dv = Ctr(s_dv)
        sc = Ctr(s_sc)
        gs = Ctr(s_gs)
        sd = [Ctr(s_d0), Ctr(s_d1), Ctr(s_d2)]

        hsu = a16.alloc("hsu", NCD * 128)
        h1sb = a32.alloc("h1sb", 128)
        h2sb = a32.alloc("h2sb", 128)
        tmp64 = a32.alloc("tmp64", NCI)

        @block.gpsimd
        def _(g):
            for dst, src in ((ident, id_d), (h1sb, h1_d), (h2sb, h2_d)):
                gp.inc(g.dma_start(out=dst, in_=src[:]), 16)
            for nm in ("bv", "bo", "suug", "sudp", "wsu", "wsg", "wsd"):
                gp.inc(g.dma_start(out=vecs[nm], in_=vec_d[nm][:]), 16)
        const_dma = gp.n

        @block.vector
        def _(v):
            v.wait_ge(s_gp, const_dma)
            for c in range(NCD):
                dv.inc(v.tensor_scalar(hsu[:, c * 128:(c + 1) * 128], h1sb,
                                       vecs["suug"][:, c:c + 1], None, ALU.mult))
            dv.inc(v.tensor_copy(hdp, h2sb))
            dv.inc(v.tensor_scalar(hdpn, h2sb, -1.0, None, ALU.mult))
            dv.inc(v.tensor_scalar(tmp64, vecs["wsu"], FWHT_K1, None, ALU.mult))
            dv.inc(v.reduce_sum(vecs["bveff"], vecs["bv"], axis=AX))
            v.wait_ge(s_dv, dv.n)
            dv.inc(v.tensor_mul(vecs["supe"], tmp64, vecs["sudp"]))
            dv.inc(v.tensor_scalar(vecs["sgpe"], vecs["wsg"], FWHT_K1, None, ALU.mult))
            dv.inc(v.tensor_scalar(vecs["sdpe"], vecs["wsd"], FWHT_K2, None, ALU.mult))
        const_dv = dv.n

        xn = a16.alloc("xn", NCD * T)
        xn_tm_b = [a16.alloc(f"xn_tm{i}", D) for i in range(2)]
        xt_b = [a32.alloc(f"xt{i}", D) for i in range(2)]
        veff = a16.alloc("veff", NCD * 128)
        woeff = [a16.alloc(f"woe{i}", D) for i in range(2)]
        vwsl = [a16.alloc(f"vwsl{i}", 2 * D) for i in range(2)]
        vtmp = a16.alloc("vtmp", 3584)
        pstr = [ps_alloc(f"pstr{i}", [128, 1024], F16) for i in range(2)]

        NPAIR = NCD // 2
        NFP = max(1, NPAIR // NT)
        assert NFP * NT == NPAIR or NT == 1
        ln_sc_done = 2 * NT
        ln_pe_done = 2 * NT

        sqrt_sc = [0] * NT
        evac1_sc = [0] * NT
        _scpos = 0
        for _t in range(NT):
            _scpos += 1
            sqrt_sc[_t] = _scpos
            if _t >= 1:
                _scpos += 1
                evac1_sc[_t - 1] = _scpos
        _scpos += 1
        evac1_sc[NT - 1] = _scpos

        eps_mark = [0] * NT
        apply_mark = [0] * NT
        woadd_mark = [0] * NPAIR
        _dvpos = const_dv
        for _t in range(NT + 1):
            if _t < NT:
                _dvpos += 6
                eps_mark[_t] = _dvpos
            if _t >= 1:
                _tt = _t - 1
                _dvpos += 3
                apply_mark[_tt] = _dvpos
                _dvpos += 1
                for _jf in range(NFP):
                    _p = _tt * NFP + _jf
                    if _p < NPAIR:
                        _dvpos += 2
                        woadd_mark[_p] = _dvpos
        ln_dv_done = _dvpos

        wo_final = woeff[(NPAIR * 2 - 1) % 2]
        vw_mark = []

        @block.gpsimd
        def _(g):
            for p in range(NPAIR):
                if p >= 1:
                    g.wait_ge(s_gs, 4 * p)
                    g.wait_ge(s_dv, woadd_mark[p - 1])
                for half in range(2):
                    k = 2 * p + half
                    sw[0].inc(g.dma_start(out=vwsl[0][:, half * D:(half + 1) * D],
                                          in_=wv_d[k * 128:(k + 1) * 128, :]), 16)
                    sw[1].inc(g.dma_start(out=vwsl[1][:, half * D:(half + 1) * D],
                                          in_=wo_d[k * 128:(k + 1) * 128, :]), 16)
                vw_mark.append((sw[0].n, sw[1].n))
                g.wait_ge(s_w0, sw[0].n)
                v0 = vwsl[0].rearrange("p (b a two d) -> p b a two d",
                                       b=2, two=2, d=128)
                vt1 = vtmp[:, 0:2048].rearrange("p (b a d) -> p b a d", b=2, d=128)
                gs.inc(g.tensor_add(vt1, v0[:, :, :, 0, :], v0[:, :, :, 1, :]))
                g.wait_ge(s_gs, gs.n)
                t1f = vtmp[:, 0:2048].rearrange("p (b a two d) -> p b a two d",
                                                b=2, two=2, d=128)
                vt2 = vtmp[:, 2048:3072].rearrange("p (b a d) -> p b a d", b=2, d=128)
                gs.inc(g.tensor_add(vt2, t1f[:, :, :, 0, :], t1f[:, :, :, 1, :]))
                g.wait_ge(s_gs, gs.n)
                t2f = vtmp[:, 2048:3072].rearrange("p (b a two d) -> p b a two d",
                                                   b=2, two=2, d=128)
                vt3 = vtmp[:, 3072:3584].rearrange("p (b a d) -> p b a d", b=2, d=128)
                gs.inc(g.tensor_add(vt3, t2f[:, :, :, 0, :], t2f[:, :, :, 1, :]))
                g.wait_ge(s_gs, gs.n)
                t3f = vtmp[:, 3072:3584].rearrange("p (b two d) -> p b two d",
                                                   b=2, d=128)
                gs.inc(g.tensor_add(
                    veff[:, 2 * p * 128:(2 * p + 2) * 128].rearrange(
                        "p (b one d) -> p b one d", one=1, d=128),
                    t3f[:, :, 0:1, :], t3f[:, :, 1:2, :]))
        veff_done = gs.n
        assert veff_done == 4 * NPAIR

        xt_mark = []

        @block.sync
        def _(s):
            for t in range(NT):
                if t >= 2:
                    s.wait_ge(s_dv, apply_mark[t - 2])
                sx[t % 2].inc(s.dma_start(out=xt_b[t % 2],
                                          in_=x_tm_d[t * 128:(t + 1) * 128, :]), 16)
                xt_mark.append((sx[t % 2].h, sx[t % 2].n))

        @block.scalar
        def _(s):
            def evac1(t):
                s.wait_ge(s_pe, 2 * (t + 1))
                dst = xn.rearrange("p (c t2) -> p c t2", c=NCD)[
                    :, 8:16, t * 128:(t + 1) * 128]
                sc.inc(s.activation(dst, pstr[1][:].rearrange(
                    "p (c t2) -> p c t2", c=8), AF.Identity))
            for t in range(NT):
                s.wait_ge(s_dv, eps_mark[t])
                sc.inc(s.activation(stat[:, 5 * t + 4:5 * t + 5],
                                    stat[:, 5 * t + 1:5 * t + 2], AF.Sqrt))
                if t >= 1:
                    evac1(t - 1)
            evac1(NT - 1)
        assert sc.n == ln_sc_done

        @block.vector
        def _(v):
            NB = D // 512
            for t in range(NT + 1):
                if t < NT:
                    v.wait_ge(*xt_mark[t])
                    for q in range(NB):
                        dv.inc(v.bn_stats(stat[:, 40 + q * 6:40 + (q + 1) * 6],
                                          xt_b[t % 2][:, q * 512:(q + 1) * 512]))
                    v.wait_ge(s_dv, dv.n)
                    dv.inc(v.bn_aggr(stat[:, 5 * t:5 * t + 2], stat[:, 40:40 + 6 * NB]))
                    v.wait_ge(s_dv, dv.n)
                    dv.inc(v.tensor_scalar(stat[:, 5 * t + 1:5 * t + 2],
                                           stat[:, 5 * t + 1:5 * t + 2], EPS, None, ALU.add))
                    assert dv.n == eps_mark[t]
                if t >= 1:
                    tt = t - 1
                    v.wait_ge(s_sc, sqrt_sc[tt])
                    dv.inc(v.reciprocal(stat[:, 5 * tt + 2:5 * tt + 3],
                                        stat[:, 5 * tt + 4:5 * tt + 5]))
                    v.wait_ge(s_dv, dv.n)
                    dv.inc(v.scalar_tensor_tensor(stat[:, 5 * tt + 3:5 * tt + 4],
                                                  stat[:, 5 * tt:5 * tt + 1], -1.0,
                                                  stat[:, 5 * tt + 2:5 * tt + 3],
                                                  ALU.mult, ALU.mult))
                    v.wait_ge(s_dv, dv.n)
                    dv.inc(v.tensor_scalar(xn_tm_b[tt % 2], xt_b[tt % 2],
                                           stat[:, 5 * tt + 2:5 * tt + 3],
                                           stat[:, 5 * tt + 3:5 * tt + 4],
                                           ALU.mult, ALU.add))
                    assert dv.n == apply_mark[tt]
                    v.wait_ge(s_pe, 2 * (tt + 1))
                    dst = xn.rearrange("p (c t2) -> p c t2", c=NCD)[
                        :, 0:8, tt * 128:(tt + 1) * 128]
                    dv.inc(v.tensor_copy(dst, pstr[0][:].rearrange(
                        "p (c t2) -> p c t2", c=8)))
                    for jf in range(NFP):
                        p = tt * NFP + jf
                        if p >= NPAIR:
                            continue
                        v.wait_ge(s_w1, vw_mark[p][1])
                        for half in range(2):
                            k = 2 * p + half
                            if k == 0:
                                dv.inc(v.tensor_copy(woeff[0], vwsl[1][:, 0:D]))
                            else:
                                v.wait_ge(s_dv, dv.n)
                                dv.inc(v.tensor_add(woeff[k % 2], woeff[(k - 1) % 2],
                                                    vwsl[1][:, half * D:(half + 1) * D]))
                        assert dv.n == woadd_mark[p]
        assert dv.n == ln_dv_done

        @block.tensor
        def _(te):
            for t in range(NT):
                te.wait_ge(s_dv, apply_mark[t])
                if t >= 1:
                    te.wait_ge(s_sc, evac1_sc[t - 1])
                for gb in range(2):
                    for c in range(8):
                        cc = gb * 8 + c
                        ins = te.transpose(pstr[gb][:, c * 128:(c + 1) * 128],
                                           xn_tm_b[t % 2][:, cc * 128:(cc + 1) * 128],
                                           ident)
                    pe.inc(ins)
        assert pe.n == ln_pe_done and dv.n == ln_dv_done and sc.n == ln_sc_done

        vsum = a16.alloc("vsum", T)
        a16.release("xn_tm0", "xn_tm1")
        a32.release("xt0", "xt1", "h1sb", "h2sb", "tmp64")
        ps_free("pstr1", "pstr0")
        psV = ps_alloc("psV", [128, T], F32)
        psO = [ps_alloc(f"psO{i}", [128, T], F32) for i in range(2)]
        psQx = ps_alloc("psQx", [128, T], F32)
        xfb = [a32.alloc(f"xfb{i}", T) for i in range(2)]
        accb = [a32.alloc(f"acb{i}", T) for i in range(2)]
        a16.release("vwsl0", "vwsl1", "vtmp")
        x1 = [a16.alloc(f"x1_{i}", T, high=True) for i in range(NCD)]
        x1 = x1 + [a16.alloc(f"sp{i}", T, high=True) for i in range(2)]

        vs_pe_done = ln_pe_done + 1
        vs_dv_done = ln_dv_done + 1
        f1_pe_done = vs_pe_done + NCD
        wo_pe_done = f1_pe_done + NCD
        f1_ev = []
        _dvx, _scx = vs_dv_done, ln_sc_done
        for c in range(NCD):
            if c % 2 == 0:
                _dvx += 1
                f1_ev.append((s_dv, _dvx))
            else:
                _scx += 1
                f1_ev.append((s_sc, _scx))
        f1_dv_done, f1_sc_done = _dvx, _scx
        acc_marks = []
        acb_order = []
        _p = f1_dv_done
        _pd = 0
        _mn = 0
        for _pair in range(2 * NCD):
            while _mn < NCD and _pd >= 2 * _mn:
                acb_order.append('A')
                _p += 1
                acc_marks.append(_p)
                _mn += 1
            acb_order.append('B')
            _p += 2
            _pd += 1
        while _mn < NCD:
            acb_order.append('A')
            _p += 1
            acc_marks.append(_p)
            _mn += 1
        f1b_dv_done = _p

        def wo_acc_mark(m):
            return acc_marks[m]

        wo_dv_done = f1b_dv_done

        @block.tensor
        def _(te):
            te.wait_ge(s_dv, ln_dv_done)
            te.wait_ge(s_sc, ln_sc_done)
            te.wait_ge(s_gs, veff_done)
            for hf in range(NH):
                for k in range(NCD):
                    ins = te.matmul(psV[:, hf * HW:(hf + 1) * HW],
                                    veff[:, k * 128:(k + 1) * 128],
                                    xn[:, k * T + hf * HW: k * T + (hf + 1) * HW],
                                    start=(k == 0), stop=(k == NCD - 1))
            pe.inc(ins)
            x1ps = [psQx, psV, psO[0], psO[1]]
            for c in range(NCD):
                if c == 1:
                    te.wait_ge(s_dv, vs_dv_done)
                if c >= 4:
                    te.wait_ge(f1_ev[c - 4][0], f1_ev[c - 4][1])
                dstp = x1ps[c % 4]
                for hf in range(NH):
                    ins = te.matmul(dstp[:, hf * HW:(hf + 1) * HW],
                                    hsu[:, c * 128:(c + 1) * 128],
                                    xn[:, c * T + hf * HW: c * T + (hf + 1) * HW],
                                    start=True, stop=True)
                pe.inc(ins)
            te.wait_ge(s_dv, vs_dv_done)
            woPS = [psO[0], psO[1], psQx, psV]
            for m in range(NCD):
                if m == 0:
                    te.wait_ge(*f1_ev[NCD - 2])
                    te.wait_ge(*f1_ev[NCD - 1])
                if m >= 4:
                    te.wait_ge(s_dv, wo_acc_mark(m - 4))
                for hf in range(NH):
                    ins = te.matmul(woPS[m % 4][:, hf * HW:(hf + 1) * HW],
                                    wo_final[:, m * 128:(m + 1) * 128],
                                    vsum[:, hf * HW:(hf + 1) * HW],
                                    start=True, stop=True)
                pe.inc(ins)
        assert pe.n == wo_pe_done

        @block.scalar
        def _(s):
            for c in range(1, NCD, 2):
                s.wait_ge(s_pe, vs_pe_done + c + 1)
                sc.inc(s.activation(x1[c], (psV if c % 4 == 1 else psO[1])[:],
                                    AF.Identity))
        assert sc.n == f1_sc_done

        x1_slots = list(range(NCD))
        x1_final_spare = []

        wo_xf_dma = []
        wo_st_mark = []

        @block.sync
        def _(s):
            s.wait_ge(s_dv, vs_dv_done)
            s.wait_ge(s_sc, ln_sc_done)
            for m in range(NCD):
                if m >= 2:
                    s.wait_ge(s_dv, wo_acc_mark(m - 2))
                sx[m % 2].inc(s.dma_start(out=xfb[m % 2],
                                          in_=x_fm_d[m * 128:(m + 1) * 128, :]), 16)
                wo_xf_dma.append((sx[m % 2].h, sx[m % 2].n))
                s.wait_ge(s_dv, wo_acc_mark(m))
                so[m % 2].inc(s.dma_start(out=scr_d[m * 128:(m + 1) * 128, :],
                                          in_=accb[m % 2]), 16)
                wo_st_mark.append((so[m % 2].h, so[m % 2].n))
        wo_so_done = (so[0].n, so[1].n)

        @block.vector
        def _(v):
            v.wait_ge(s_pe, vs_pe_done)
            dv.inc(v.tensor_scalar(vsum, psV[:], vecs["bveff"], None, ALU.add))
            for c in range(0, NCD, 2):
                v.wait_ge(s_pe, vs_pe_done + c + 1)
                dv.inc(v.tensor_copy(x1[c], (psQx if c % 4 == 0
                                             else psO[0])[:]))
            assert dv.n == f1_dv_done
            v.wait_ge(s_sc, f1_sc_done)
            woPS = [psO[0], psO[1], psQx, psV]
            pairs = [(st, c) for st in range(4) for c in range(NCD)
                     if not c & (1 << st)]
            spare = [NCD, NCD + 1]
            ai, pi = 0, 0
            for tok in acb_order:
                if tok == 'A':
                    m = ai
                    ai += 1
                    v.wait_ge(s_pe, f1_pe_done + m + 1)
                    v.wait_ge(*wo_xf_dma[m])
                    if m >= 2:
                        v.wait_ge(*wo_st_mark[m - 2])
                    dv.inc(v.scalar_tensor_tensor(accb[m % 2],
                                                  woPS[m % 4][:],
                                                  vecs["bo"][:, m:m + 1],
                                                  xfb[m % 2], ALU.add, ALU.add))
                    assert dv.n == acc_marks[m]
                else:
                    st, c = pairs[pi]
                    pi += 1
                    step = 1 << st
                    a, b = x1_slots[c], x1_slots[c | step]
                    fa, fb = spare
                    v.wait_ge(s_dv, dv.n)
                    dv.inc(v.tensor_add(x1[fa], x1[a], x1[b]))
                    dv.inc(v.tensor_sub(x1[fb], x1[a], x1[b]))
                    spare = [a, b]
                    x1_slots[c], x1_slots[c | step] = fa, fb
            x1_final_spare.extend(spare)
            assert dv.n == f1b_dv_done


        x1_free_names = [(f"x1_{ix}" if ix < NCD else f"sp{ix - NCD}")
                         for ix in x1_final_spare]

        a16.release("xn", "hsu", "veff", "woe0", "woe1", "vsum", *x1_free_names)
        hbuf = [a16.alloc(f"h_{i}", T, high=True) for i in range(NCI)]
        hbuf = hbuf + [a16.alloc(f"hsp{i}", T, high=True) for i in range(2)]
        uslab = [a16.alloc(f"usl{i}", NCD * 128) for i in range(2)]
        gslab = [a16.alloc(f"gsl{i}", NCD * 128) for i in range(2)]
        sgb = [a16.alloc(f"sgb{i}", T, high=True) for i in range(2)]
        ps_free("psQx", "psO1", "psO0", "psV")
        psU = [ps_alloc(f"psU{i}", [128, 2 * T], F32) for i in range(2)]

        ug_slab_dma = []

        bfly_sched = {i: [] for i in range(NCI)}
        for s5 in range(5):
            step5 = 1 << s5
            for c5 in range(NCI):
                if c5 & step5:
                    continue
                bfly_sched[c5 | (2 * step5 - 1)].append((s5, c5))
        hmul_mark = []
        hmul_only = []
        _m = wo_dv_done
        for i in range(NCI):
            _m += 1
            hmul_only.append(_m)
            _m += 2 * len(bfly_sched[i])
            hmul_mark.append(_m)

        def ug_silu(i):
            return f1_sc_done + i + 1

        upg_base = wo_pe_done

        ug_dv_done = hmul_mark[-1]
        ug_sc_done = f1_sc_done + NCI
        ug_pe_done = wo_pe_done + NCI

        @block.gpsimd
        def _(g):
            g.wait_ge(s_pe, wo_pe_done)
            g.wait_ge(s_dv, wo_dv_done)
            for i in range(NCI):
                if i >= 2:
                    g.wait_ge(s_pe, wo_pe_done + i - 1)
                sw[i % 2].inc(g.dma_start(out=uslab[i % 2],
                                          in_=wup_d[i * 128:(i + 1) * 128, :]), 16)
                sw[i % 2].inc(g.dma_start(out=gslab[i % 2],
                                          in_=wgt_d[i * 128:(i + 1) * 128, :]), 16)
                ug_slab_dma.append((sw[i % 2].h, sw[i % 2].n))

        @block.tensor
        def _(te):
            te.wait_ge(s_dv, wo_dv_done)
            te.wait_ge(s_sc, f1_sc_done)
            for i in range(NCI):
                par = i % 2
                te.wait_ge(*ug_slab_dma[i])
                if i >= 2:
                    te.wait_ge(s_sc, ug_silu(i - 2))
                    te.wait_ge(s_dv, hmul_only[i - 2])
                for hf in range(NH):
                    for k in range(NCD):
                        te.matmul(psU[par][:, hf * HW:(hf + 1) * HW],
                                  uslab[par][:, k * 128:(k + 1) * 128],
                                  x1[x1_slots[k]][:, hf * HW:(hf + 1) * HW],
                                  start=(k == 0), stop=(k == NCD - 1))
                for hf in range(NH):
                    for k in range(NCD):
                        ins = te.matmul(psU[par][:, T + hf * HW: T + (hf + 1) * HW],
                                        gslab[par][:, k * 128:(k + 1) * 128],
                                        x1[x1_slots[k]][:, hf * HW:(hf + 1) * HW],
                                        start=(k == 0), stop=(k == NCD - 1))
                pe.inc(ins)

        @block.scalar
        def _(s):
            for i in range(NCI):
                par = i % 2
                s.wait_ge(s_pe, upg_base + i + 1)
                if i >= 2:
                    s.wait_ge(s_dv, hmul_only[i - 2])
                sc.inc(s.activation(sgb[par], psU[par][:, T:2 * T], AF.Silu,
                                    scale=vecs["sgpe"][:, i:i + 1]))

        h_slots = list(range(NCI))
        h_level = [0] * NCI
        cross_order = []

        @block.vector
        def _(v):
            spare = [NCI, NCI + 1]
            for i in range(NCI):
                par = i % 2
                v.wait_ge(s_pe, upg_base + i + 1)
                v.wait_ge(s_sc, ug_silu(i))
                dv.inc(v.scalar_tensor_tensor(hbuf[i], psU[par][:, 0:T],
                                              vecs["supe"][:, i:i + 1],
                                              sgb[par], ALU.mult, ALU.mult))
                for (s5, c5) in bfly_sched[i]:
                    d5 = c5 | (1 << s5)
                    a, b = h_slots[c5], h_slots[d5]
                    fa, fb = spare
                    v.wait_ge(s_dv, dv.n)
                    dv.inc(v.tensor_add(hbuf[fa], hbuf[a], hbuf[b]))
                    dv.inc(v.tensor_sub(hbuf[fb], hbuf[a], hbuf[b]))
                    spare = [a, b]
                    h_slots[c5], h_slots[d5] = fa, fb
                    h_level[c5] += 1
                    h_level[d5] += 1
                    for cc in (c5, d5):
                        if h_level[cc] == 5:
                            cross_order.append((cc, dv.n))
                assert dv.n == hmul_mark[i]
        assert dv.n == ug_dv_done and sc.n == ug_sc_done and pe.n == ug_pe_done
        assert len(cross_order) == NCI and all(l == 5 for l in h_level)

        for nm6 in ([f"x1_{i2}" for i2 in range(NCD)] + ["sp0", "sp1"]):
            if nm6 not in x1_free_names:
                a16.release(nm6)

        f2_ev = []
        _dvx, _scx = ug_dv_done, ug_sc_done
        for c in range(NCI):
            if c % 2 == 0:
                _dvx += 1
            else:
                _scx += 1
            f2_ev.append((_dvx, _scx))
        f2_dv_done, f2_sc_done = _dvx, _scx
        f2_pe_done = ug_pe_done + NCI

        F2DEPTH = 4 if T >= 512 else 2

        def psq4(j):
            if F2DEPTH == 4:
                return psU[j % 2][:, ((j // 2) % 2) * T:(((j // 2) % 2) + 1) * T]
            return psU[j % 2][:, 0:T]

        cmark = {c: mk for c, mk in cross_order}
        pairs = sorted(range(NCI // 2), key=lambda p: max(cmark[p], cmark[p + 32]))
        out_chunks = []
        for p in pairs:
            out_chunks.append((p, max(cmark[p], cmark[p + 32]), +1))
            out_chunks.append((p + 32, max(cmark[p], cmark[p + 32]), -1))

        @block.tensor
        def _(te):
            te.wait_ge(s_sc, ug_sc_done)
            for j in range(NCI):
                c, mk, sgn = out_chunks[j]
                p = c % 32
                te.wait_ge(s_dv, mk)
                if j >= F2DEPTH:
                    te.wait_ge(s_dv, f2_ev[j - F2DEPTH][0])
                    te.wait_ge(s_sc, f2_ev[j - F2DEPTH][1])
                dst = psq4(j)
                for hf in range(NH):
                    te.matmul(dst[:, hf * HW:(hf + 1) * HW],
                              hdp,
                              hbuf[h_slots[p]][:, hf * HW:(hf + 1) * HW],
                              start=True, stop=False)
                    ins = te.matmul(dst[:, hf * HW:(hf + 1) * HW],
                                    hdp if sgn > 0 else hdpn,
                                    hbuf[h_slots[p + 32]][:, hf * HW:(hf + 1) * HW],
                                    start=False, stop=True)
                pe.inc(ins)

        @block.vector
        def _(v):
            for j in range(0, NCI, 2):
                v.wait_ge(s_pe, ug_pe_done + (j | 1) + 1)
                dv.inc(v.tensor_copy(hbuf[h_slots[out_chunks[j][0]]], psq4(j)))

        @block.scalar
        def _(s):
            for j in range(1, NCI, 2):
                s.wait_ge(s_pe, ug_pe_done + j + 1)
                sc.inc(s.activation(hbuf[h_slots[out_chunks[j][0]]], psq4(j), AF.Identity))
        assert dv.n == f2_dv_done and sc.n == f2_sc_done and pe.n == f2_pe_done
        f2b_dv_done = f2_dv_done

        a16.release("usl0", "usl1", "gsl0", "gsl1", "sgb0", "sgb1")
        a32.release("xfb0", "xfb1", "acb0", "acb1")
        dslab = [a16.alloc(f"dsl{i}", 16 * 128) for i in range(12)]
        xfb2 = [a32.alloc(f"xf2{i}", T) for i in range(2)]
        outb = [a32.alloc(f"out{i}", T) for i in range(2)]

        dn_slab_dma = []
        dn_acc_dma = []
        dn_st_mark = []

        def dn_fin(m):
            return f2b_dv_done + m + 1

        @block.gpsimd
        def _(g):
            g.wait_ge(s_dv, ug_dv_done)
            g.wait_ge(s_pe, ug_pe_done)
            for m in range(NCD):
                if m >= 3:
                    g.wait_ge(s_pe, f2_pe_done + m - 2)
                for kq in range(4):
                    idx = m * 4 + kq
                    sd[m % 3].inc(g.dma_start(
                        out=dslab[(m % 3) * 4 + kq],
                        in_=wdn_d[idx * 128:(idx + 1) * 128, :]), 16)
                dn_slab_dma.append((sd[m % 3].h, sd[m % 3].n))

        @block.sync
        def _(s):
            s.wait_ge(s_dv, f2b_dv_done)
            s.wait_ge(s_sc, f2_sc_done)
            for m in range(NCD):
                if m >= 2:
                    s.wait_ge(s_dv, dn_fin(m - 2))
                sx[m % 2].inc(s.dma_start(out=xfb2[m % 2],
                                          in_=scr_d[m * 128:(m + 1) * 128, :]), 16)
                dn_acc_dma.append((sx[m % 2].h, sx[m % 2].n))
                s.wait_ge(s_dv, dn_fin(m))
                so[m % 2].inc(s.dma_start(out=y_d[m * 128:(m + 1) * 128, :],
                                          in_=outb[m % 2]), 16)
                dn_st_mark.append((so[m % 2].h, so[m % 2].n))
            s.wait_ge(s_o0, so[0].n)
            s.wait_ge(s_o1, so[1].n)

        @block.tensor
        def _(te):
            te.wait_ge(s_dv, f2b_dv_done)
            te.wait_ge(s_sc, f2_sc_done)
            for m in range(NCD):
                par = m % 2
                if m >= 2:
                    te.wait_ge(s_dv, dn_fin(m - 2))
                te.wait_ge(*dn_slab_dma[m])
                for hf in range(NH):
                    for kq in range(4):
                        for k in range(16):
                            kk = kq * 16 + k
                            ins = te.matmul(
                                psU[par][:, hf * HW:(hf + 1) * HW],
                                dslab[(m % 3) * 4 + kq][:, k * 128:(k + 1) * 128],
                                hbuf[h_slots[kk]][:, hf * HW:(hf + 1) * HW],
                                start=(kq == 0 and k == 0), stop=(kq == 3 and k == 15))
                pe.inc(ins)

        @block.vector
        def _(v):
            v.wait_ge(s_o0, wo_so_done[0])
            v.wait_ge(s_o1, wo_so_done[1])
            for m in range(NCD):
                par = m % 2
                v.wait_ge(s_pe, f2_pe_done + m + 1)
                v.wait_ge(*dn_acc_dma[m])
                if m >= 2:
                    v.wait_ge(*dn_st_mark[m - 2])
                dv.inc(v.scalar_tensor_tensor(outb[m % 2],
                                              psU[par][:, 0:T],
                                              vecs["sdpe"][:, m:m + 1],
                                              xfb2[m % 2], ALU.mult, ALU.add))
        ps_free("psU1", "psU0")

    return nc


def _tile_upgate(w):
    wt = np.ascontiguousarray(w.T)
    t = wt.reshape(16, 128, 64, 128).transpose(2, 1, 0, 3)
    return np.ascontiguousarray(t).reshape(64 * 128, 16 * 128)


def _tile_down(w):
    wt = np.ascontiguousarray(w.T)
    t = wt.reshape(4, 16, 128, 16, 128).transpose(3, 0, 2, 1, 4)
    return np.ascontiguousarray(t).reshape(64 * 128, 16 * 128)


def _hadamard(n):
    h = np.array([[1.0]], dtype=np.float64)
    while h.shape[0] < n:
        h = np.block([[h, h], [h, -h]])
    return h


def _vec(v, nch):
    return np.ascontiguousarray(np.asarray(v, np.float32).reshape(nch, 128).T)


def kernel(L0_attn_x, L0_attn_Wq, L0_attn_bq, L0_attn_Wk, L0_attn_bk,
           L0_attn_Wv, L0_attn_bv, L0_attn_Wo, L0_attn_bo,
           L0_ffn_SU_ug, L0_ffn_SU_dp, L0_ffn_Wscale_ug, L0_ffn_Wscale_dp,
           L0_ffn_W_up, L0_ffn_W_gate, L0_ffn_W_down):
    x = np.ascontiguousarray(np.asarray(L0_attn_x, dtype=np.float32)).reshape(-1, D)
    n_tok = x.shape[0]
    T = n_tok // N_CORES

    ws = np.asarray(L0_ffn_Wscale_ug, np.float32)
    shared = dict(
        wv=np.ascontiguousarray(np.asarray(L0_attn_Wv, np.float32).T),
        wo=np.ascontiguousarray(np.asarray(L0_attn_Wo, np.float32).T),
        wup=_tile_upgate(np.asarray(L0_ffn_W_up, np.float32)),
        wgt=_tile_upgate(np.asarray(L0_ffn_W_gate, np.float32)),
        wdn=_tile_down(np.asarray(L0_ffn_W_down, np.float32)),
        bv=_vec(L0_attn_bv, NCD), bo=_vec(L0_attn_bo, NCD),
        suug=_vec(L0_ffn_SU_ug, NCD), sudp=_vec(L0_ffn_SU_dp, NCI),
        wsu=_vec(ws[:I], NCI), wsg=_vec(ws[I:], NCI),
        wsd=_vec(L0_ffn_Wscale_dp, NCD),
        h1c=(_hadamard(128) * FWHT_A).astype(np.float32),
        h2c=(_hadamard(128) * FWHT_A).astype(np.float32),
        identc=np.eye(128, dtype=np.float16),
    )

    nc = build_program(T)
    in_maps = []
    for ci in range(N_CORES):
        xs = x[ci * T:(ci + 1) * T]
        m = dict(shared)
        m["x_tm"] = np.ascontiguousarray(xs)
        m["x_fm"] = np.ascontiguousarray(xs.T)
        in_maps.append(m)

    res = run_bass_kernel_spmd(nc, in_maps, core_ids=list(range(N_CORES)))
    y = np.concatenate([res.results[ci]["y"] for ci in range(N_CORES)], axis=1)
    return np.ascontiguousarray(y.T).reshape(np.asarray(L0_attn_x).shape).astype(np.float32)
